# version 01
# baseline (speedup 1.0000x reference)
import numpy as np
import ml_dtypes
from contextlib import ExitStack

import concourse.bass as bass
import concourse.mybir as mybir
from concourse.bass_utils import run_bass_kernel_spmd

F32 = mybir.dt.float32
BF16 = mybir.dt.bfloat16
AF = mybir.ActivationFunctionType
ALU = mybir.AluOpType
AX = mybir.AxisListType

NCORES = 8
T = 4096
D = 1024
NTOK = 32768
NH = 16
DK = 64
SCALE_L0 = 1.0 / float(np.sqrt(np.float32(64 * 3)))
SCALE_L1 = 0.125
F0 = 2730
F0P = 2816
NF0 = 22
H1 = 2048
FF = 4096
W_EXT = 1536
PAD = 256

ENGINES = ("sync", "gpsimd", "tensor", "vector", "scalar")


class SemT:
    def __init__(self, h):
        self.h = h
        self.n = 0

    def inc(self, k=1):
        self.n += k
        return (self.h, k)

    def tgt(self):
        return (self.h, self.n)

    def at(self, v):
        return (self.h, v)


class Ctx:

    SELF_ENGINES = ("vector", "scalar")

    def __init__(self, nc, st):
        self.nc = nc
        self.st = st
        self.prog = {e: [] for e in ENGINES}
        self._sems = {}
        self.cnt = {e: 0 for e in self.SELF_ENGINES}
        self.in_names = set()

    def sem(self, name):
        if name not in self._sems:
            h = self.st.enter_context(self.nc.semaphore(name))
            self._sems[name] = SemT(h)
        return self._sems[name]

    def sb(self, name, shape, dt):
        return self.st.enter_context(self.nc.sbuf_tensor(name, list(shape), dt))

    def ps(self, name, shape, dt=F32):
        return self.st.enter_context(self.nc.psum_tensor(name, list(shape), dt))

    def bank(self, i):
        if not hasattr(self, "_banks"):
            self._banks = [None] * 8
        if self._banks[i] is None:
            self._banks[i] = self.st.enter_context(
                self.nc.psum_tensor("bank%d" % i, [128, 512], F32))
        return self._banks[i]

    def dram(self, name, shape, dt, kind="Internal"):
        if kind == "ExternalInput":
            self.in_names.add(name)
        return self.nc.dram_tensor(name, list(shape), dt, kind=kind)

    def op(self, eng, fn, wait=(), inc=()):
        if eng in self.SELF_ENGINES:
            assert not inc, "no explicit incs on self-serialized engines"
            self.cnt[eng] += 1
            self.prog[eng].append((fn, list(wait), []))
            return (eng, self.cnt[eng])
        self.prog[eng].append((fn, list(wait), list(inc)))
        return None

    def tok(self, eng):
        return (eng, self.cnt[eng])

    def wait(self, eng, waits):
        self.prog[eng].append((None, list(waits), []))


def build_launch(emit_fn, *args, **kwargs):
    nc = bass.Bass("TRN2", target_bir_lowering=False)
    with ExitStack() as st:
        ctx = Ctx(nc, st)
        emit_fn(ctx, *args, **kwargs)
        selfsems = {e: st.enter_context(nc.semaphore(e + "_self"))
                    for e in Ctx.SELF_ENGINES}

        def resolve(w):
            if isinstance(w[0], str):
                return (selfsems[w[0]], w[1])
            return w

        block = st.enter_context(nc.Block())
        for ename in ENGINES:
            items = ctx.prog[ename]
            ss = selfsems.get(ename)

            def make_body(items, ss):
                def body(eng):
                    n_done = 0
                    for fn, waits, incs in items:
                        for w in waits:
                            h, v = resolve(w)
                            eng.wait_ge(h, v)
                        if fn is None:
                            continue
                        if ss is not None and n_done > 0:
                            eng.wait_ge(ss, n_done)
                        ins = fn(eng)
                        if ss is not None:
                            ins.then_inc(ss, 1)
                            n_done += 1
                        for (h, a) in incs:
                            ins.then_inc(h, a)
                return body

            getattr(block, ename)(make_body(items, ss))
    return nc, sorted(ctx.in_names)



def _t(a):
    return np.ascontiguousarray(np.asarray(a).T)


def _cols(v):
    v = np.asarray(v).reshape(-1, 128)
    return np.ascontiguousarray(v.T)


def prep_inputs(inputs):
    S = {}
    x = np.asarray(inputs["L0_attn_x"])
    xt = np.ascontiguousarray(x.reshape(NTOK, D).T)

    S["l0_wqT"] = _t(inputs["L0_attn_Wq"])
    S["l0_wkT"] = _t(inputs["L0_attn_Wk"])
    S["l0_wvT"] = _t(inputs["L0_attn_Wv"])
    S["l0_woT"] = _t(inputs["L0_attn_Wo"])
    S["l0_wpkT"] = _t(inputs["L0_attn_Wpk"])
    S["l0_wpqT"] = _t(inputs["L0_attn_Wpq"])
    S["l0_bq"] = _cols(inputs["L0_attn_bq"])
    S["l0_bk"] = _cols(inputs["L0_attn_bk"])
    S["l0_bv"] = _cols(inputs["L0_attn_bv"])
    S["l0_bo"] = _cols(inputs["L0_attn_bo"])
    S["l0_bpk"] = _cols(inputs["L0_attn_bpk"])
    S["l0_bpq"] = _cols(inputs["L0_attn_bpq"])
    S["l0_relT"] = _t(inputs["L0_attn_rel_emb"])
    S["l0_rel_rT"] = np.ascontiguousarray(np.asarray(inputs["L0_attn_rel_emb"])[::-1].T)

    S["l0_wr"] = np.ascontiguousarray(np.asarray(inputs["L0_ffn_Wr"]))
    S["l0_br"] = np.ascontiguousarray(np.asarray(inputs["L0_ffn_br"]).reshape(8, 1))
    wu = np.asarray(inputs["L0_ffn_W_up"])
    wup = np.zeros((4, D, F0P), np.float32); wup[:, :, :F0] = wu
    S["l0_wup"] = wup
    bu = np.asarray(inputs["L0_ffn_b_up"])
    bup = np.zeros((4, F0P), np.float32); bup[:, :F0] = bu
    S["l0_bup"] = np.ascontiguousarray(bup.reshape(4, NF0, 128).transpose(0, 2, 1))
    wd = np.asarray(inputs["L0_ffn_W_down"])
    wdp = np.zeros((8, F0P, D), np.float32); wdp[:, :F0, :] = wd
    S["l0_wdn"] = wdp
    S["l0_bdn"] = np.ascontiguousarray(np.asarray(inputs["L0_ffn_b_down"]).reshape(1, 8 * D))

    S["l1_sfwT"] = _t(inputs["L1_attn_sf_w"])
    S["l1_maskT"] = _t(inputs["L1_attn_mask"])
    S["l1_sfb"] = _cols(inputs["L1_attn_sf_b"])
    S["l1_wqT"] = _t(inputs["L1_attn_wq"])
    S["l1_wkT"] = _t(inputs["L1_attn_wk"])
    S["l1_wvT"] = _t(inputs["L1_attn_wv"])
    S["l1_woT"] = _t(inputs["L1_attn_wo"])
    S["l1_bq"] = _cols(inputs["L1_attn_bq"])
    S["l1_bk"] = _cols(inputs["L1_attn_bk"])
    S["l1_bv"] = _cols(inputs["L1_attn_bv"])
    S["l1_bo"] = _cols(inputs["L1_attn_bo"])
    S["l1_ff1T"] = _t(inputs["L1_attn_ff1_w"])
    S["l1_ff1b"] = _cols(inputs["L1_attn_ff1_b"])
    S["l1_ff2T"] = _t(inputs["L1_attn_ff2_w"])
    S["l1_ff2b"] = _cols(inputs["L1_attn_ff2_b"])
    for nm in ("g1", "b1", "g2", "b2", "g3", "b3"):
        S["l1_" + nm] = _cols(inputs["L1_attn_" + nm])

    S["l1_rwT"] = _t(inputs["L1_ffn_router_w"])
    S["l1_wg"] = np.ascontiguousarray(np.asarray(inputs["L1_ffn_w_gate"]))
    S["l1_wu"] = np.ascontiguousarray(np.asarray(inputs["L1_ffn_w_up"]))
    S["l1_wd"] = np.ascontiguousarray(np.asarray(inputs["L1_ffn_w_down"]))

    S["c_id_bf"] = np.eye(128, dtype=ml_dtypes.bfloat16)
    S["c_id_f32"] = np.eye(128, dtype=np.float32)
    S["c_invD_bf"] = np.full((128, 128), 1.0 / 1024.0, dtype=ml_dtypes.bfloat16)
    S["c_onescol_bf"] = np.ones((128, 1), dtype=ml_dtypes.bfloat16)
    S["c_ones1x128_f32"] = np.ones((1, 128), dtype=np.float32)
    S["c_ones1x8_f32"] = np.ones((1, 8), dtype=np.float32)
    S["c_ones_row_bf"] = np.ones((1, T), dtype=ml_dtypes.bfloat16)
    S["c_zeros_row_bf"] = np.zeros((1, T), dtype=ml_dtypes.bfloat16)
    cm = np.zeros((4, 128, 512), np.float32)
    for v in range(4):
        for ki in range(128):
            cm[v, ki, 128 * v + ki:] = 1.0
    S["c_causal"] = cm

    per_core = []
    for c in range(NCORES):
        rot = np.concatenate([xt[:, 4096 * c:], xt[:, :4096 * c]], axis=1)
        per_core.append({"xT_all": np.ascontiguousarray(rot)})
    return S, per_core


COMMON_SPECS = {
    "xT_all": ((D, NTOK), F32),
    "l0_wqT": ((D, D), F32), "l0_wkT": ((D, D), F32), "l0_wvT": ((D, D), F32),
    "l0_woT": ((D, D), F32), "l0_wpkT": ((D, D), F32), "l0_wpqT": ((D, D), F32),
    "l0_bq": ((128, 8), F32), "l0_bk": ((128, 8), F32), "l0_bv": ((128, 8), F32),
    "l0_bo": ((128, 8), F32), "l0_bpk": ((128, 8), F32), "l0_bpq": ((128, 8), F32),
    "l0_relT": ((D, D), F32), "l0_rel_rT": ((D, D), F32),
    "l0_wr": ((D, 8), F32), "l0_br": ((8, 1), F32),
    "l0_wup": ((4, D, F0P), F32), "l0_bup": ((4, 128, NF0), F32),
    "l0_wdn": ((8, F0P, D), F32), "l0_bdn": ((1, 8 * D), F32),
    "l1_sfwT": ((D, D), F32), "l1_maskT": ((D, D), F32), "l1_sfb": ((128, 8), F32),
    "l1_wqT": ((D, D), F32), "l1_wkT": ((D, D), F32), "l1_wvT": ((D, D), F32),
    "l1_woT": ((D, D), F32),
    "l1_bq": ((128, 8), F32), "l1_bk": ((128, 8), F32), "l1_bv": ((128, 8), F32),
    "l1_bo": ((128, 8), F32),
    "l1_ff1T": ((D, FF), F32), "l1_ff1b": ((128, 32), F32),
    "l1_ff2T": ((FF, D), F32), "l1_ff2b": ((128, 8), F32),
    "l1_g1": ((128, 8), F32), "l1_b1": ((128, 8), F32),
    "l1_g2": ((128, 8), F32), "l1_b2": ((128, 8), F32),
    "l1_g3": ((128, 8), F32), "l1_b3": ((128, 8), F32),
    "l1_rwT": ((D, 8), F32),
    "l1_wg": ((8, D, H1), F32), "l1_wu": ((8, D, H1), F32), "l1_wd": ((8, H1, D), F32),
    "c_id_bf": ((128, 128), BF16), "c_id_f32": ((128, 128), F32),
    "c_invD_bf": ((128, 128), BF16), "c_onescol_bf": ((128, 1), BF16),
    "c_ones1x128_f32": ((1, 128), F32), "c_ones1x8_f32": ((1, 8), F32),
    "c_ones_row_bf": ((1, T), BF16), "c_zeros_row_bf": ((1, T), BF16),
    "c_causal": ((4, 128, 512), F32),
}


def declare_common(ctx, names=None):
    d = {}
    for nm, (shape, dt) in COMMON_SPECS.items():
        if names is not None and nm not in names:
            continue
        d[nm] = ctx.dram(nm, shape, dt, kind="ExternalInput").ap()
    return d



def emit_ln(ctx, io, src32, dst_bf, dst_f32=None, gain=None, bias=None, tag="ln"):
    ld = ctx.sem(tag + "_ld")
    gld = ctx.sem(tag + "_gld")
    mm = ctx.sem(tag + "_mm")
    stv = ctx.sem(tag + "_st")

    xb = ctx.sb(tag + "_xb", (128, 2 * 4096), BF16)
    x32 = ctx.sb(tag + "_x32", (128, 2 * 4096), F32)
    xsq = ctx.sb(tag + "_xsq", (128, 2 * 4096), BF16)
    invD = ctx.sb(tag + "_invD", (128, 128), BF16)
    ones1x128 = ctx.sb(tag + "_o1x128", (1, 128), F32)
    mrow = ctx.sb(tag + "_mrow", (1, 512), F32)
    sqrow = ctx.sb(tag + "_sqrow", (1, 512), F32)
    m2row = ctx.sb(tag + "_m2row", (1, 512), F32)
    varrow = ctx.sb(tag + "_varrow", (1, 512), F32)
    lnrow = ctx.sb(tag + "_lnrow", (1, 512), F32)
    rrow = ctx.sb(tag + "_rrow", (1, 512), F32)
    mrrow = ctx.sb(tag + "_mrrow", (1, 512), F32)
    tt = ctx.sb(tag + "_tt", (128, 512), F32)
    outb = ctx.sb(tag + "_outb", (128, 4096), BF16)
    need32 = dst_f32 is not None
    out32 = ctx.sb(tag + "_out32", (128, 4096), F32) if need32 else None
    gb = ctx.sb(tag + "_g", (128, 8), F32) if gain is not None else None
    bbt = ctx.sb(tag + "_b", (128, 8), F32) if bias is not None else None

    ps_m = ctx.bank(0)
    ps_sq = ctx.bank(1)
    ps_r = ctx.bank(2)
    ps_mr = ctx.bank(3)

    ctx.op("sync", lambda e: e.dma_start(out=invD[:], in_=io["c_invD_bf"][:, :]), inc=[ld.inc(16)])
    ctx.op("sync", lambda e: e.dma_start(out=ones1x128[:], in_=io["c_ones1x128_f32"][:, :]), inc=[ld.inc(16)])
    if gain is not None:
        ctx.op("sync", lambda e: e.dma_start(out=gb[:], in_=gain[:, :]), inc=[ld.inc(16)])
    if bias is not None:
        ctx.op("sync", lambda e: e.dma_start(out=bbt[:], in_=bias[:, :]), inc=[ld.inc(16)])
    const_t = ld.tgt()

    src_r = src32.rearrange("(dt p) t -> p dt t", p=128)
    dstb_r = dst_bf.rearrange("(dt p) t -> p dt t", p=128)
    dst32_r = dst_f32.rearrange("(dt p) t -> p dt t", p=128) if need32 else None

    done = [ctx.tok("vector"), ctx.tok("scalar")]
    for j in range(8):
        db = j % 2
        c0 = 512 * j
        war = [done[j]] if j >= 2 else []
        ctx.op("gpsimd", lambda e, db=db, c0=c0: e.dma_start(
            out=bass.AP(xb, db * 4096, [[8192, 128], [512, 8], [1, 512]]),
            in_=src_r[:, :, c0:c0 + 512]), wait=war, inc=[gld.inc(16)])
        gld_t = gld.tgt()
        ctx.op("sync", lambda e, db=db, c0=c0: e.dma_start(
            out=bass.AP(x32, db * 4096, [[8192, 128], [512, 8], [1, 512]]),
            in_=src_r[:, :, c0:c0 + 512]), wait=war, inc=[ld.inc(16)])
        ld_t = ld.tgt()

        sq_t = ctx.op("vector", lambda e, db=db: e.tensor_tensor(
            xsq[:, db * 4096:(db + 1) * 4096], xb[:, db * 4096:(db + 1) * 4096],
            xb[:, db * 4096:(db + 1) * 4096], op=ALU.mult), wait=[gld_t])

        for dt in range(8):
            ctx.op("tensor", lambda e, db=db, dt=dt: e.matmul(
                ps_m[:, :], invD[:, :], xb[:, db * 4096 + dt * 512: db * 4096 + (dt + 1) * 512],
                start=(dt == 0), stop=(dt == 7)),
                wait=([gld_t, const_t, done[j], done[j + 1]] if dt == 0 else []))
        for dt in range(8):
            ctx.op("tensor", lambda e, db=db, dt=dt: e.matmul(
                ps_sq[:, :], invD[:, :], xsq[:, db * 4096 + dt * 512: db * 4096 + (dt + 1) * 512],
                start=(dt == 0), stop=(dt == 7)),
                wait=([sq_t] if dt == 0 else []),
                inc=([mm.inc()] if dt == 7 else []))
        stats_t = mm.tgt()

        ctx.op("vector", lambda e: e.tensor_copy(mrow[:], ps_m[0:1, :]), wait=[stats_t])
        ctx.op("vector", lambda e: e.tensor_copy(sqrow[:], ps_sq[0:1, :]))
        ctx.op("vector", lambda e: e.tensor_tensor(m2row[:], mrow[:], mrow[:], op=ALU.mult))
        ctx.op("vector", lambda e: e.tensor_tensor(varrow[:], sqrow[:], m2row[:], op=ALU.subtract))
        var_t = ctx.op("vector", lambda e: e.tensor_scalar(varrow[:], varrow[:], 1e-5, None, op0=ALU.add))
        ctx.op("scalar", lambda e: e.activation(lnrow[:], varrow[:], AF.Ln), wait=[var_t])
        r_t = ctx.op("scalar", lambda e: e.activation(rrow[:], lnrow[:], AF.Exp, scale=-0.5))
        mr_t = ctx.op("vector", lambda e: e.tensor_tensor(mrrow[:], mrow[:], rrow[:], op=ALU.mult), wait=[r_t])

        ctx.op("tensor", lambda e: e.matmul(ps_r[:, :], ones1x128[0:1, :], rrow[0:1, :],
                                            start=True, stop=True), wait=[mr_t])
        ctx.op("tensor", lambda e: e.matmul(ps_mr[:, :], ones1x128[0:1, :], mrrow[0:1, :],
                                            start=True, stop=True), inc=[mm.inc()])
        bc_t = mm.tgt()

        last_sc = None
        for dt in range(8):
            ctx.op("vector", lambda e, db=db, dt=dt: e.tensor_tensor(
                tt[:, :], x32[:, db * 4096 + dt * 512: db * 4096 + (dt + 1) * 512],
                ps_r[:, :], op=ALU.mult), wait=([bc_t, ld_t] if dt == 0 else []))
            dst_slice = out32[:, dt * 512:(dt + 1) * 512] if need32 else outb[:, dt * 512:(dt + 1) * 512]
            nrm_t = ctx.op("vector", lambda e, dst_slice=dst_slice: e.tensor_tensor(
                dst_slice, tt[:, :], ps_mr[:, :], op=ALU.subtract))
            if need32 or gain is not None:
                srcn = out32[:, dt * 512:(dt + 1) * 512] if need32 else outb[:, dt * 512:(dt + 1) * 512]
                if gain is not None:
                    last_sc = ctx.op("scalar", lambda e, dt=dt, srcn=srcn: e.activation(
                        outb[:, dt * 512:(dt + 1) * 512], srcn, AF.Identity,
                        bias=bbt[:, dt:dt + 1], scale=gb[:, dt:dt + 1]), wait=[nrm_t])
                else:
                    last_sc = ctx.op("scalar", lambda e, dt=dt, srcn=srcn: e.copy(
                        outb[:, dt * 512:(dt + 1) * 512], srcn), wait=[nrm_t])
        norm_done = [ctx.tok("vector")] + ([last_sc] if last_sc else [])

        ctx.op("sync", lambda e, c0=c0: e.dma_start(
            out=dstb_r[:, :, c0:c0 + 512],
            in_=bass.AP(outb, 0, [[4096, 128], [512, 8], [1, 512]])),
            wait=norm_done, inc=[stv.inc(16)])
        if need32:
            ctx.op("sync", lambda e, c0=c0: e.dma_start(
                out=dst32_r[:, :, c0:c0 + 512],
                in_=bass.AP(out32, 0, [[4096, 128], [512, 8], [1, 512]])),
                wait=norm_done, inc=[stv.inc(16)])
        done.append(ctx.tok("vector"))
        ctx.wait("vector", [stv.tgt()])
        ctx.wait("scalar", [stv.tgt()])
    ctx.wait("sync", [stv.tgt()])
    return stv


def emit_router0(ctx, io, ge_dram, thr_dbg=None, dbg=None, tag="r0"):
    ld = ctx.sem(tag + "_ld")
    gld = ctx.sem(tag + "_gld")
    mm = ctx.sem(tag + "_mm")
    stv = ctx.sem(tag + "_st")

    xb = ctx.sb(tag + "_xb", (128, 2 * 4096), BF16)
    xsq = ctx.sb(tag + "_xsq", (128, 2 * 4096), BF16)
    onescol = ctx.sb(tag + "_ocol", (128, 1), BF16)
    onescol_f = ctx.sb(tag + "_ocolf", (128, 1), F32)
    ones1x128 = ctx.sb(tag + "_o1x128", (1, 128), F32)
    ones1x8 = ctx.sb(tag + "_o1x8", (1, 8), F32)
    wr = ctx.sb(tag + "_wr", (128, 8 * 8), BF16)
    br = ctx.sb(tag + "_br", (8, 1), F32)
    sneg = ctx.sb(tag + "_sneg", (1, 8), F32)
    idf = ctx.sb(tag + "_idf", (128, 128), F32)

    mrow = ctx.sb(tag + "_mrow", (1, 512), F32)
    sqrow = ctx.sb(tag + "_sqrow", (1, 512), F32)
    m2row = ctx.sb(tag + "_m2row", (1, 512), F32)
    varrow = ctx.sb(tag + "_varrow", (1, 512), F32)
    lnrow = ctx.sb(tag + "_lnrow", (1, 512), F32)
    rrow = ctx.sb(tag + "_rrow", (1, 512), F32)
    lg1 = ctx.sb(tag + "_lg1", (8, 512), F32)
    r8 = ctx.sb(tag + "_r8", (8, 512), F32)
    lg2 = ctx.sb(tag + "_lg2", (8, 512), F32)
    lg3 = ctx.sb(tag + "_lg3", (8, 512), F32)
    logits_tok = ctx.sb(tag + "_ltok", (128, 2048), F32)
    exp_tok = ctx.sb(tag + "_etok", (128, 2048), F32)
    sum8 = ctx.sb(tag + "_sum8", (128, 256), F32)
    rec8 = ctx.sb(tag + "_rec8", (128, 256), F32)
    sm = ctx.sb(tag + "_sm", (128, 2048), F32)
    sm2 = ctx.sb(tag + "_sm2", (128, 2048), F32)
    cmp = ctx.sb(tag + "_cmp", (128, 2048), F32)
    cntp = ctx.sb(tag + "_cntp", (128, 8), F32)
    thr_bc = ctx.sb(tag + "_thrbc", (128, 8), F32)
    lo = [ctx.sb(tag + "_lo0", (1, 8), F32), ctx.sb(tag + "_lo1", (1, 8), F32)]
    hi = [ctx.sb(tag + "_hi0", (1, 8), F32), ctx.sb(tag + "_hi1", (1, 8), F32)]
    mid = ctx.sb(tag + "_mid", (1, 8), F32)
    tmp8a = ctx.sb(tag + "_t8a", (1, 8), F32)
    tmp8b = ctx.sb(tag + "_t8b", (1, 8), F32)
    tmp8c = ctx.sb(tag + "_t8c", (1, 8), F32)
    flag = ctx.sb(tag + "_flag", (1, 8), F32)
    flagc = ctx.sb(tag + "_flagc", (1, 8), F32)
    mask_own = ctx.sb(tag + "_maskown", (128, 256), F32)
    ge_own = ctx.sb(tag + "_geown", (128, 256), F32)
    ge_tr = ctx.sb(tag + "_getr", (32, 128), F32)
    cnt_sb = ctx.sb(tag + "_cntsb", (1, 8), F32)

    b4, b5, b6, b7 = ctx.bank(4), ctx.bank(5), ctx.bank(6), ctx.bank(7)
    ps_m = b4[0:1, :]
    ps_s = b4[0:1, 0:8]
    ps_sq = b5[0:1, :]
    ps_lg = b6[0:8, :]
    ps_r8 = b6[0:8, :]
    ps_cnt = b6[0:1, 0:8]
    ps_tr = b7[0:128, 0:8]
    ps_thr = b7[0:128, 0:8]
    ps_getr = b7[0:32, 0:128]

    ctx.op("sync", lambda e: e.dma_start(out=onescol[:], in_=io["c_onescol_bf"][:, :]), inc=[ld.inc(16)])
    ctx.op("sync", lambda e: e.dma_start(out=ones1x128[:], in_=io["c_ones1x128_f32"][:, :]), inc=[ld.inc(16)])
    ctx.op("sync", lambda e: e.dma_start(out=ones1x8[:], in_=io["c_ones1x8_f32"][:, :]), inc=[ld.inc(16)])
    ctx.op("sync", lambda e: e.dma_start(out=br[:], in_=io["l0_br"][:, :]), inc=[ld.inc(16)])
    ctx.op("sync", lambda e: e.dma_start(out=idf[:], in_=io["c_id_f32"][:, :]), inc=[ld.inc(16)])
    ctx.op("gpsimd", lambda e: e.dma_start(
        out=bass.AP(wr, 0, [[64, 128], [8, 8], [1, 8]]),
        in_=io["l0_wr"].rearrange("(ct p) e -> p ct e", p=128)), inc=[gld.inc(16)])
    const_t = ld.tgt()
    wr_t = gld.tgt()

    ocf_t = ctx.op("vector", lambda e: e.tensor_copy(onescol_f[:], onescol[:]), wait=[const_t])

    for ct in range(8):
        ctx.op("tensor", lambda e, ct=ct: e.matmul(
            ps_s[:, :], onescol[:, :], wr[:, ct * 8:(ct + 1) * 8],
            start=(ct == 0), stop=(ct == 7)), wait=([wr_t, const_t] if ct == 0 else []),
            inc=([mm.inc()] if ct == 7 else []))
    sneg_t = ctx.op("vector", lambda e: e.tensor_scalar(sneg[:], ps_s[0:1, :], -1.0, None, op0=ALU.mult),
                    wait=[mm.tgt()])

    xsrc = io["xT_all"].rearrange("(dt p) t -> p dt t", p=128)
    done = []
    for j in range(64):
        db = j % 2
        c0 = 512 * j
        war = [done[j - 2]] if j >= 2 else []
        ctx.op("gpsimd", lambda e, db=db, c0=c0: e.dma_start(
            out=bass.AP(xb, db * 4096, [[8192, 128], [512, 8], [1, 512]]),
            in_=xsrc[:, :, c0:c0 + 512]), wait=war, inc=[gld.inc(16)])
        x_t = gld.tgt()
        sq_t = ctx.op("vector", lambda e, db=db: e.tensor_tensor(
            xsq[:, db * 4096:(db + 1) * 4096], xb[:, db * 4096:(db + 1) * 4096],
            xb[:, db * 4096:(db + 1) * 4096], op=ALU.mult), wait=[x_t])

        for dt in range(8):
            ctx.op("tensor", lambda e, db=db, dt=dt: e.matmul(
                ps_m[:, :], onescol[:, 0:1], xb[:, db * 4096 + dt * 512: db * 4096 + (dt + 1) * 512],
                start=(dt == 0), stop=(dt == 7)),
                wait=(([x_t, const_t] + [done[j - 1] if j >= 1 else sneg_t]) if dt == 0 else []))
        for dt in range(8):
            ctx.op("tensor", lambda e, db=db, dt=dt: e.matmul(
                ps_sq[:, :], onescol[:, 0:1], xsq[:, db * 4096 + dt * 512: db * 4096 + (dt + 1) * 512],
                start=(dt == 0), stop=(dt == 7)), wait=([sq_t] if dt == 0 else []))
        for dt in range(8):
            ctx.op("tensor", lambda e, db=db, dt=dt: e.matmul(
                ps_lg[:, :], wr[:, dt * 8:(dt + 1) * 8],
                xb[:, db * 4096 + dt * 512: db * 4096 + (dt + 1) * 512],
                start=(dt == 0), stop=False), wait=([wr_t] if dt == 0 else []),
                inc=([mm.inc()] if dt == 7 else []))
        stats_t = mm.tgt()

        ctx.op("vector", lambda e: e.tensor_scalar(mrow[:], ps_m[0:1, :], 1.0 / 1024.0, None, op0=ALU.mult),
               wait=[stats_t])
        ctx.op("vector", lambda e: e.tensor_scalar(sqrow[:], ps_sq[0:1, :], 1.0 / 1024.0, None, op0=ALU.mult))
        ctx.op("vector", lambda e: e.tensor_tensor(m2row[:], mrow[:], mrow[:], op=ALU.mult))
        ctx.op("vector", lambda e: e.tensor_tensor(varrow[:], sqrow[:], m2row[:], op=ALU.subtract))
        var_t = ctx.op("vector", lambda e: e.tensor_scalar(varrow[:], varrow[:], 1e-5, None, op0=ALU.add))
        ctx.op("scalar", lambda e: e.activation(lnrow[:], varrow[:], AF.Ln), wait=[var_t])
        r_t = ctx.op("scalar", lambda e: e.activation(rrow[:], lnrow[:], AF.Exp, scale=-0.5))
        mrow_used = ctx.tok("vector")

        ctx.op("tensor", lambda e: e.matmul(ps_lg[:, :], sneg[0:1, :], mrow[0:1, :],
                                            start=False, stop=True),
               wait=[sneg_t, mrow_used], inc=[mm.inc()])
        lgmm_t = mm.tgt()
        lg1_t = ctx.op("vector", lambda e: e.tensor_copy(lg1[:], ps_lg[:, :]), wait=[lgmm_t])
        ctx.op("tensor", lambda e: e.matmul(ps_r8[:, :], ones1x8[0:1, :], rrow[0:1, :],
                                            start=True, stop=True), wait=[r_t, lg1_t], inc=[mm.inc()])
        r8_t = mm.tgt()
        ctx.op("vector", lambda e: e.tensor_copy(r8[:], ps_r8[:, :]), wait=[r8_t])
        ctx.op("vector", lambda e: e.tensor_tensor(lg2[:], lg1[:], r8[:], op=ALU.mult))
        lg_t = ctx.op("vector", lambda e: e.tensor_scalar(lg3[:], lg2[:], br[:, 0:1], None, op0=ALU.add))

        for i in range(4):
            tw = [lg_t, const_t] if i == 0 else [ctx.tok("vector")]
            ctx.op("tensor", lambda e, i=i: e.transpose(
                ps_tr[:, :], lg3[:, i * 128:(i + 1) * 128], idf[0:8, 0:8]),
                wait=tw, inc=[mm.inc()])
            tr_t = mm.tgt()
            jj = 4 * j + i
            ctx.op("vector", lambda e, jj=jj: e.tensor_copy(
                logits_tok[:, jj * 8:(jj + 1) * 8], ps_tr[:, :]), wait=[tr_t])
        done.append(ctx.tok("vector"))

    all_lg = ctx.tok("vector")

    exp_t = ctx.op("scalar", lambda e: e.activation(exp_tok[:], logits_tok[:], AF.Exp), wait=[all_lg])
    ctx.op("vector", lambda e: e.tensor_reduce(
        sum8[:, :], bass.AP(exp_tok, 0, [[2048, 128], [8, 256], [1, 8]]), axis=AX.X, op=ALU.add),
        wait=[exp_t])
    ctx.op("vector", lambda e: e.reciprocal(rec8[:], sum8[:]))
    recb = bass.AP(rec8, 0, [[256, 128], [1, 256], [0, 8]])
    sm_t = ctx.op("vector", lambda e: e.tensor_tensor(sm[:], exp_tok[:], recb, op=ALU.mult))
    smr = bass.AP(sm, 0, [[2048, 128], [1, 8], [8, 256]])
    sm2_t = ctx.op("vector", lambda e: e.tensor_copy(sm2[:], smr))

    ctx.op("vector", lambda e: e.memset(lo[0][:], 0.0))
    ctx.op("vector", lambda e: e.memset(hi[0][:], 1.0))
    for it in range(40):
        a, b = it % 2, (it + 1) % 2
        ctx.op("vector", lambda e, a=a: e.tensor_tensor(tmp8a[:], lo[a][:], hi[a][:], op=ALU.add))
        mid_t = ctx.op("vector", lambda e: e.tensor_scalar(mid[:], tmp8a[:], 0.5, None, op0=ALU.mult))
        ctx.op("tensor", lambda e: e.matmul(ps_thr[:, :], ones1x128[0:1, :], mid[0:1, :],
                                            start=True, stop=True), wait=[mid_t], inc=[mm.inc()])
        thr_mm = mm.tgt()
        ctx.op("vector", lambda e: e.tensor_copy(thr_bc[:], ps_thr[:, :]), wait=[thr_mm])
        thrb = bass.AP(thr_bc, 0, [[8, 128], [1, 8], [0, 256]])
        ctx.op("vector", lambda e, thrb=thrb: e.tensor_tensor(cmp[:], sm2[:], thrb, op=ALU.is_gt))
        cnt_t = ctx.op("vector", lambda e: e.tensor_reduce(
            cntp[:, :], bass.AP(cmp, 0, [[2048, 128], [256, 8], [1, 256]]), axis=AX.X, op=ALU.add))
        ctx.op("tensor", lambda e: e.matmul(ps_cnt[:, :], onescol_f[:, 0:1], cntp[:, :],
                                            start=True, stop=True), wait=[cnt_t, ocf_t], inc=[mm.inc()])
        cnt_mm = mm.tgt()
        ctx.op("vector", lambda e: e.tensor_scalar(flag[:], ps_cnt[0:1, :], 4095.5, None, op0=ALU.is_gt),
               wait=[cnt_mm])
        ctx.op("vector", lambda e: e.tensor_scalar(flagc[:], flag[:], -1.0, 1.0, op0=ALU.mult, op1=ALU.add))
        ctx.op("vector", lambda e, a=a: e.tensor_tensor(tmp8a[:], mid[:], lo[a][:], op=ALU.subtract))
        ctx.op("vector", lambda e: e.tensor_tensor(tmp8b[:], tmp8a[:], flag[:], op=ALU.mult))
        ctx.op("vector", lambda e, a=a, b=b: e.tensor_tensor(lo[b][:], lo[a][:], tmp8b[:], op=ALU.add))
        ctx.op("vector", lambda e, a=a: e.tensor_tensor(tmp8a[:], mid[:], hi[a][:], op=ALU.subtract))
        ctx.op("vector", lambda e: e.tensor_tensor(tmp8c[:], tmp8a[:], flagc[:], op=ALU.mult))
        upd_t = ctx.op("vector", lambda e, a=a, b=b: e.tensor_tensor(hi[b][:], hi[a][:], tmp8c[:], op=ALU.add))
        if dbg is not None:
            ctx.op("sync", lambda e, it=it: e.dma_start(out=dbg["mids"][it:it+1, :], in_=mid[:]),
                   wait=[upd_t], inc=[stv.inc(16)])
            cs_t = ctx.op("vector", lambda e: e.tensor_copy(cnt_sb[:], ps_cnt[0:1, :]))
            ctx.op("sync", lambda e, it=it: e.dma_start(out=dbg["rawcnts"][it:it+1, :], in_=cnt_sb[:]),
                   wait=[cs_t], inc=[stv.inc(16)])
    fin = 40 % 2
    if dbg is not None:
        ctx.op("sync", lambda e: e.dma_start(out=dbg["ltok"][:, :], in_=logits_tok[:]), wait=[all_lg], inc=[stv.inc(16)])
        ctx.op("sync", lambda e: e.dma_start(out=dbg["smd"][:, :], in_=sm[:]), wait=[sm_t], inc=[stv.inc(16)])
    ctx.op("tensor", lambda e, fin=fin: e.matmul(ps_thr[:, :], ones1x128[0:1, :], lo[fin][0:1, :],
                                                 start=True, stop=True), wait=[ctx.tok("vector")], inc=[mm.inc()])
    thr_t = ctx.op("vector", lambda e: e.tensor_copy(thr_bc[:], ps_thr[:, :]), wait=[mm.tgt()])

    own32 = bass.AP(sm, 0, [[2048, 128], [8, 32], [1, 8]])
    thrb2 = bass.AP(thr_bc, 0, [[8, 128], [0, 32], [1, 8]])
    ctx.op("vector", lambda e: e.tensor_tensor(mask_own[:], own32, thrb2, op=ALU.is_gt), wait=[thr_t])
    ge_t = ctx.op("vector", lambda e: e.tensor_tensor(ge_own[:], own32, mask_own[:], op=ALU.mult))
    if thr_dbg is not None:
        ctx.op("sync", lambda e, fin=fin: e.dma_start(out=thr_dbg[:, :], in_=lo[fin][:]),
               wait=[thr_t], inc=[stv.inc(16)])
    for e_i in range(8):
        src = bass.AP(ge_own, e_i, [[256, 128], [8, 32]])
        ctx.op("tensor", lambda e, src=src: e.transpose(ps_getr[:, :], src, idf[:, :]),
               wait=[ge_t] + ([stv.tgt()] if e_i > 0 else []), inc=[mm.inc()])
        t_t = mm.tgt()
        c_t = ctx.op("vector", lambda e: e.tensor_copy(ge_tr[:], ps_getr[:, :]), wait=[t_t])
        ctx.op("sync", lambda e, e_i=e_i: e.dma_start(
            out=ge_dram[e_i:e_i + 1, :], in_=ge_tr[:]), wait=[c_t], inc=[stv.inc(16)])
    ctx.wait("sync", [stv.tgt()])
    return stv



def emit_l0attn(ctx, io, xn0_dram, s0_dram, tag="a0", max_pairs=8, phase=5, subp=3):
    ld = ctx.sem(tag + "_ld")
    gld = ctx.sem(tag + "_gld")
    trd = ctx.sem(tag + "_trd")
    mm = ctx.sem(tag + "_mm")
    stv = ctx.sem(tag + "_st")

    xn0 = ctx.sb(tag + "_xn0", (128, 8 * 4096), BF16)
    wslab = ctx.sb(tag + "_wslab", (128, 2 * 1024), BF16)
    relc = ctx.sb(tag + "_relc", (128, 2 * 1024), BF16)
    relrc = ctx.sb(tag + "_relrc", (128, 2 * 1024), BF16)
    wpk_t = ctx.sb(tag + "_wpk", (128, 2 * 128), BF16)
    wpq_t = ctx.sb(tag + "_wpq", (128, 2 * 128), BF16)
    posk = ctx.sb(tag + "_posk", (128, 1024), BF16)
    posq = ctx.sb(tag + "_posq", (128, 1024), BF16)
    QT = ctx.sb(tag + "_QT", (128, 4096), BF16)
    KT = ctx.sb(tag + "_KT", (128, 4096), BF16)
    VT = ctx.sb(tag + "_VT", (128, 4096), BF16)
    vtok = ctx.sb(tag + "_vtok", (128, 2 * 32 * 65), BF16)
    Qa = ctx.sb(tag + "_Qa", (128, 4096), BF16)
    Kl = ctx.sb(tag + "_Kl", (128, 4096), BF16)
    Kr = ctx.sb(tag + "_Kr", (128, 4096), BF16)
    edgeq = ctx.sb(tag + "_edgeq", (2, 4096), BF16)
    edgek = ctx.sb(tag + "_edgek", (2, 4096), BF16)
    cpev = ctx.sb(tag + "_cpev", (128, 1536), BF16)
    etmp = ctx.sb(tag + "_etmp", (128, 2), BF16)
    strip = ctx.sb(tag + "_strip", (128, 2 * 1408), BF16)
    Pt = ctx.sb(tag + "_Pt", (128, 2 * 4096), BF16)
    sc = ctx.sb(tag + "_sc", (128, 2 * 512), F32)
    ctxA = ctx.sb(tag + "_ctxA", (65, 4096), F32)
    recs = ctx.sb(tag + "_recs", (1, 512), F32)
    ctxev = ctx.sb(tag + "_ctxev", (64, 512), BF16)
    ctxld = None
    onespad = ctx.sb(tag + "_onespad", (128, 256), BF16)
    id_bf = ctx.sb(tag + "_idbf", (128, 128), BF16)
    o128 = ctx.sb(tag + "_o128", (1, 128), F32)
    sbq = ctx.sb(tag + "_sbq", (128, 8), F32)
    bkc = ctx.sb(tag + "_bkc", (128, 8), F32)
    bvc = ctx.sb(tag + "_bvc", (128, 8), F32)
    boc = ctx.sb(tag + "_boc", (128, 8), F32)
    bpkc = ctx.sb(tag + "_bpkc", (128, 8), F32)
    sbpq = ctx.sb(tag + "_sbpq", (128, 8), F32)
    s0ev = ctx.sb(tag + "_s0ev", (128, 512), F32)

    c2p_arr = ctx.dram(tag + "_c2p", (NH * T, W_EXT), BF16).ap()
    p2c_arr = ctx.dram(tag + "_p2c", (NH * T, W_EXT), BF16).ap()
    ctxT_all = ctx.dram(tag + "_ctxT", (D, T), BF16).ap()
    kedge = ctx.dram(tag + "_kedge", (NH, 2, T), BF16).ap()

    b = [ctx.bank(i) for i in range(8)]

    for nm, dst in [("c_id_bf", id_bf), ("l0_bq", sbq), ("l0_bk", bkc), ("l0_bv", bvc),
                    ("l0_bo", boc), ("l0_bpk", bpkc), ("l0_bpq", sbpq),
                    ("c_ones1x128_f32", o128)]:
        ctx.op("sync", lambda e, nm=nm, dst=dst: e.dma_start(out=dst[:], in_=io[nm][:, :]), inc=[ld.inc(16)])
    ctx.op("sync", lambda e: e.dma_start(out=Qa[66:67, :], in_=io["c_ones_row_bf"][:, :]), inc=[ld.inc(16)])
    ctx.op("sync", lambda e: e.dma_start(out=Kl[64:65, :], in_=io["c_ones_row_bf"][:, :]), inc=[ld.inc(16)])
    ctx.op("sync", lambda e: e.dma_start(out=Kl[65:66, :], in_=io["c_zeros_row_bf"][:, :]), inc=[ld.inc(16)])
    ctx.op("sync", lambda e: e.dma_start(out=Kr[64:65, :], in_=io["c_zeros_row_bf"][:, :]), inc=[ld.inc(16)])
    ctx.op("sync", lambda e: e.dma_start(out=Kr[65:66, :], in_=io["c_ones_row_bf"][:, :]), inc=[ld.inc(16)])
    const_t = ld.tgt()
    ctx.op("vector", lambda e: e.tensor_scalar(sbq[:], sbq[:], SCALE_L0, None, op0=ALU.mult), wait=[const_t])
    sc_bias_t = ctx.op("vector", lambda e: e.tensor_scalar(sbpq[:], sbpq[:], SCALE_L0, None, op0=ALU.mult))
    ones_t = ctx.op("vector", lambda e: e.memset(onespad[:], 1.0))
    xn_r = xn0_dram.rearrange("(dt p) t -> p dt t", p=128)
    ctx.op("sync", lambda e: e.dma_start(
        out=bass.AP(xn0, 0, [[32768, 128], [4096, 8], [1, 4096]]), in_=xn_r[:, :, :]),
        inc=[ld.inc(16)])
    xn_t = ld.tgt()

    ws = {"q": io["l0_wqT"], "k": io["l0_wkT"], "v": io["l0_wvT"]}
    evp = {"q": (SCALE_L0, sbq), "k": (1.0, bkc), "v": (1.0, bvc)}

    def band_range(k0):
        glo = max(0, ((k0 - 510) // 128) * 128)
        ghi = min(T, ((k0 + 638) // 128) * 128 + 128)
        return glo, ghi

    for pair in range(max_pairs):
        for ci in range(8):
            db = ci % 2
            war = [mm.at(mm.n)]
            ctx.op("gpsimd", lambda e, ci=ci, db=db: e.dma_start(
                out=relc[:, db * 1024:(db + 1) * 1024], in_=io["l0_relT"][128 * ci:128 * (ci + 1), :]),
                wait=war, inc=[gld.inc(16)])
            ctx.op("gpsimd", lambda e, ci=ci, db=db: e.dma_start(
                out=relrc[:, db * 1024:(db + 1) * 1024], in_=io["l0_rel_rT"][128 * ci:128 * (ci + 1), :]),
                inc=[gld.inc(16)])
            ctx.op("gpsimd", lambda e, ci=ci, db=db, pair=pair: e.dma_start(
                out=wpk_t[:, db * 128:(db + 1) * 128],
                in_=io["l0_wpkT"][128 * ci:128 * (ci + 1), 128 * pair:128 * (pair + 1)]),
                inc=[gld.inc(16)])
            ctx.op("gpsimd", lambda e, ci=ci, db=db, pair=pair: e.dma_start(
                out=wpq_t[:, db * 128:(db + 1) * 128],
                in_=io["l0_wpqT"][128 * ci:128 * (ci + 1), 128 * pair:128 * (pair + 1)]),
                inc=[gld.inc(16)])
            g_t = gld.tgt()
            pw = [g_t] + ([ctx.tok("scalar"), ctx.tok("vector")] if ci == 0 else [])
            for half in range(2):
                ctx.op("tensor", lambda e, ci=ci, db=db, half=half: e.matmul(
                    b[half][:, :], wpk_t[:, db * 128:(db + 1) * 128],
                    relc[:, db * 1024 + half * 512: db * 1024 + (half + 1) * 512],
                    start=(ci == 0), stop=(ci == 7)),
                    wait=(pw if half == 0 else []))
            for half in range(2):
                ctx.op("tensor", lambda e, ci=ci, db=db, half=half: e.matmul(
                    b[2 + half][:, :], wpq_t[:, db * 128:(db + 1) * 128],
                    relrc[:, db * 1024 + half * 512: db * 1024 + (half + 1) * 512],
                    start=(ci == 0), stop=(ci == 7)),
                    inc=([mm.inc()] if (ci == 7 and half == 1) else []))
        pos_mm = mm.tgt()
        for half in range(2):
            ctx.op("scalar", lambda e, half=half, pair=pair: e.activation(
                posk[:, half * 512:(half + 1) * 512], b[half][:, :], AF.Identity,
                bias=bpkc[:, pair:pair + 1], scale=1.0), wait=([pos_mm] if half == 0 else []))
        for half in range(2):
            ctx.op("scalar", lambda e, half=half, pair=pair: e.activation(
                posq[:, half * 512:(half + 1) * 512], b[2 + half][:, :], AF.Identity,
                bias=sbpq[:, pair:pair + 1], scale=SCALE_L0), wait=([sc_bias_t] if half == 0 else []))
        pos_t = ctx.tok("scalar")

        proj_done = {}
        for mi, mat in enumerate(("q", "k", "v")):
            dbw = mi % 2
            ctx.op("gpsimd", lambda e, mat=mat, dbw=dbw, pair=pair: e.dma_start(
                out=bass.AP(wslab, dbw * 1024, [[2048, 128], [128, 8], [1, 128]]),
                in_=ws[mat].rearrange("(ct p) f -> p ct f", p=128)[:, :, 128 * pair:128 * (pair + 1)]),
                wait=[mm.at(mm.n)], inc=[gld.inc(16)])
            w_t = gld.tgt()
            scale, bias = evp[mat]
            dst = {"q": QT, "k": KT, "v": VT}[mat]
            for tk in range(8):
                bk_i = 4 + (tk % 2)
                fw = [w_t, xn_t, ctx.tok("scalar"), ctx.tok("vector")]
                for ci in range(8):
                    ctx.op("tensor", lambda e, dbw=dbw, tk=tk, ci=ci, bk_i=bk_i: e.matmul(
                        b[bk_i][:, :], wslab[:, dbw * 1024 + ci * 128:dbw * 1024 + (ci + 1) * 128],
                        xn0[:, ci * 4096 + tk * 512: ci * 4096 + (tk + 1) * 512],
                        start=(ci == 0), stop=(ci == 7)),
                        wait=(fw if ci == 0 else []),
                        inc=([mm.inc()] if ci == 7 else []))
                pj_mm = mm.tgt()
                ctx.op("scalar", lambda e, dst=dst, tk=tk, bk_i=bk_i, scale=scale, bias=bias, pair=pair: e.activation(
                    dst[:, tk * 512:(tk + 1) * 512], b[bk_i][:, :], AF.Identity,
                    bias=bias[:, pair:pair + 1], scale=scale), wait=[pj_mm])
            proj_done[mat] = ctx.tok("scalar")

        ctx.op("vector", lambda e: e.memset(
            bass.AP(vtok, 64, [[4160, 128], [65, 64]]), 1.0), wait=[proj_done["v"]])
        for h01 in range(2):
            for kt in range(32):
                blk = h01 * 32 + kt
                pbf = b[6][:, 0:32].bitcast(BF16)
                ctx.op("tensor", lambda e, h01=h01, kt=kt, pbf=pbf: e.transpose(
                    pbf, VT[64 * h01:64 * h01 + 64, 128 * kt:128 * (kt + 1)],
                    id_bf[64 * h01:64 * h01 + 64, 64 * h01:64 * h01 + 64]),
                    wait=[ctx.tok("vector"), const_t] if (h01 == 0 and kt == 0) else [ctx.tok("vector")],
                    inc=[mm.inc()])
                v_mm = mm.tgt()
                ctx.op("vector", lambda e, blk=blk, pbf=pbf: e.tensor_copy(
                    vtok[:, blk * 65: blk * 65 + 64], pbf), wait=[v_mm])
        vtok_t = ctx.tok("vector")

        if phase < 2:
            continue
        for h01 in range(2):
            h = 2 * pair + h01

            def emit_pos_tiles(which, stat_src, pos_src, edge_dst, arr, n_tiles=32):
                cpev_st = [stv.at(0), stv.at(0)]
                for ti in range(n_tiles):
                    db = 0
                    st_sl = stat_src[64 * h01:64 * h01 + 64, 128 * ti:128 * (ti + 1)]
                    fw = [proj_done["q" if which == "c2p" else "k"], pos_t,
                          ctx.tok("scalar"), ctx.tok("vector")]
                    for half in range(2):
                        ctx.op("tensor", lambda e, st_sl=st_sl, half=half, h01=h01: e.matmul(
                            b[half][:, :], st_sl,
                            pos_src[64 * h01:64 * h01 + 64, half * 512:(half + 1) * 512],
                            start=True, stop=True, tile_position=(64 * h01, 0)),
                            wait=(fw if half == 0 else []),
                            inc=([mm.inc()] if half == 1 else []))
                    cp_mm = mm.tgt()
                    for half in range(2):
                        ctx.op("scalar", lambda e, db=db, half=half: e.activation(
                            cpev[:, db * 1536 + 256 + half * 512: db * 1536 + 256 + (half + 1) * 512],
                            b[half][:, :], AF.Copy),
                            wait=([cp_mm, cpev_st[db]] if half == 0 else []))
                    ev_t = ctx.tok("scalar")
                    ctx.op("vector", lambda e, db=db: e.tensor_copy(
                        cpev[:, db * 1536: db * 1536 + 256],
                        bass.AP(cpev, db * 1536 + 256, [[1536, 128], [0, 256]])),
                        wait=[ev_t])
                    ctx.op("vector", lambda e, db=db: e.tensor_copy(
                        cpev[:, db * 1536 + 1280: db * 1536 + 1536],
                        bass.AP(cpev, db * 1536 + 1279, [[1536, 128], [0, 256]])))
                    ctx.op("vector", lambda e, db=db: e.tensor_copy(
                        etmp[:, :], bass.AP(cpev, db * 1536 + 256, [[1536, 128], [1023, 2]])))
                    pads_t = ctx.tok("vector")
                    ebf = b[7][0:2, 0:64].bitcast(BF16)
                    ctx.op("tensor", lambda e, ebf=ebf: e.transpose(
                        ebf, etmp[:, :], id_bf[:, :]), wait=[pads_t], inc=[mm.inc()])
                    etr_mm = mm.tgt()
                    ctx.op("vector", lambda e, ti=ti, edge_dst=edge_dst, ebf=ebf: e.tensor_copy(
                        edge_dst[:, 128 * ti:128 * (ti + 1)], ebf), wait=[etr_mm])
                    ctx.op("sync", lambda e, db=db, h=h, ti=ti, arr=arr: e.dma_start(
                        out=arr[h * T + 128 * ti: h * T + 128 * (ti + 1), :],
                        in_=cpev[:, db * 1536:(db + 1) * 1536]), wait=[pads_t], inc=[stv.inc(16)])
                    cpev_st[db] = stv.tgt()
                return ctx.tok("vector")

            emit_pos_tiles("c2p", QT, posk, edgeq, c2p_arr)
            edgek_t = emit_pos_tiles("p2c", KT, posq, edgek, p2c_arr)
            ctx.op("sync", lambda e, h=h: e.dma_start(
                out=kedge[h, :, :], in_=edgek[:, :]), wait=[edgek_t], inc=[stv.inc(16)])
            kedge_st = stv.tgt()

            asm_wait = [proj_done["q"], proj_done["k"], mm.at(mm.n)]
            ctx.op("vector", lambda e, h01=h01: e.tensor_copy(Qa[0:64, :], QT[64 * h01:64 * h01 + 64, :]),
                   wait=asm_wait)
            ctx.op("vector", lambda e: e.tensor_copy(Qa[64:66, :], edgeq[:, :]))
            ctx.op("vector", lambda e, h01=h01: e.tensor_copy(Kl[0:64, :], KT[64 * h01:64 * h01 + 64, :]))
            asm_t = ctx.op("vector", lambda e, h01=h01: e.tensor_copy(Kr[0:64, :], KT[64 * h01:64 * h01 + 64, :]))
            ctx.op("sync", lambda e, h=h: e.dma_start(
                out=Kl[66:67, :], in_=kedge[h, 1:2, :]), wait=[kedge_st, asm_t], inc=[ld.inc(16)])
            ctx.op("sync", lambda e, h=h: e.dma_start(
                out=Kr[66:67, :], in_=kedge[h, 0:1, :]), wait=[kedge_st], inc=[ld.inc(16)])
            asm_ld = ld.tgt()
            if phase < 3:
                continue

            strip_used = [None, None]
            pt_used = [None, None]
            bank_used = {i: (ctx.tok("scalar"), ctx.tok("vector")) for i in range(4)}
            sc_used = [ctx.tok("scalar"), ctx.tok("scalar")]
            ctx_acc_t = ctx.tok("vector")
            first_group = True
            for kg in range(16):
                for kk in range(2):
                    kt = 2 * kg + kk
                    k0 = 128 * kt
                    db = kt % 2
                    glo, ghi = band_range(k0)
                    wdt = ghi - glo
                    gw = ([strip_used[db]] if strip_used[db] is not None else []) + [stv.at(stv.n)]
                    ctx.op("sync", lambda e, h=h, k0=k0, glo=glo, wdt=wdt, db=db: e.dma_start_transpose(
                        strip[:, db * 1408: db * 1408 + wdt],
                        bass.AP(c2p_arr.tensor, (h * T + glo) * W_EXT + k0 - glo + 768,
                                [[W_EXT - 1, wdt], [1, 128]])),
                        wait=gw, inc=[trd.inc(16)])
                    tr_t = trd.tgt()
                    ctx.op("gpsimd", lambda e, h=h, k0=k0, glo=glo, wdt=wdt, db=db: e.dma_start(
                        out=strip[:, db * 1408: db * 1408 + wdt],
                        in_=bass.AP(p2c_arr.tensor, (h * T + k0) * W_EXT + 767 - k0 + glo,
                                    [[W_EXT - 1, 128], [1, wdt]]),
                        accum_op=ALU.add), wait=[tr_t], inc=[gld.inc(16)])
                    gat_t = gld.tgt()
                    if subp < 2:
                        continue

                    for ch in range(8):
                        c0 = 512 * ch
                        bk_i = ch % 2
                        bs = max(c0, glo)
                        be = min(c0 + 512, ghi)
                        segs = []
                        if bs > c0:
                            segs.append((c0, min(bs, c0 + 512), "R"))
                        if be > bs:
                            segs.append((bs, be, "B"))
                        if be < c0 + 512:
                            segs.append((max(be, c0), c0 + 512, "L"))
                        segs = [sg for sg in segs if sg[1] > sg[0]]
                        seg_banks = []
                        for si, (a, bb2, kind) in enumerate(segs):
                            sb_i = bk_i if si == 0 else 2 + bk_i
                            seg_banks.append(sb_i)
                            bu = bank_used.get(sb_i)
                            waits = [asm_ld] + ([t for t in bu if t is not None] if bu else [])
                            if kind == "B":
                                ctx.op("tensor", lambda e, a=a, bb2=bb2, k0=k0, sb_i=sb_i: e.matmul(
                                    b[sb_i][:, 0:bb2 - a], Kl[0:64, k0:k0 + 128],
                                    Qa[0:64, a:bb2], start=True, stop=True),
                                    wait=waits, inc=[mm.inc()])
                            else:
                                stat = Kl if kind == "L" else Kr
                                ctx.op("tensor", lambda e, a=a, bb2=bb2, k0=k0, sb_i=sb_i, stat=stat: e.matmul(
                                    b[sb_i][:, 0:bb2 - a], stat[0:67, k0:k0 + 128],
                                    Qa[0:67, a:bb2], start=True, stop=True),
                                    wait=waits, inc=[mm.inc()])
                            sc_mm = mm.tgt()
                            if kind == "B":
                                vw = [sc_mm, gat_t]
                                if sc_used[bk_i] is not None:
                                    vw.append(sc_used[bk_i])
                                v_t = ctx.op("vector", lambda e, a=a, bb2=bb2, glo=glo, db=db, sb_i=sb_i, bk_i=bk_i: e.tensor_tensor(
                                    sc[:, bk_i * 512: bk_i * 512 + bb2 - a],
                                    b[sb_i][:, 0:bb2 - a],
                                    strip[:, db * 1408 + a - glo: db * 1408 + bb2 - glo], op=ALU.add),
                                    wait=vw)
                                strip_used[db] = v_t
                                s_t = ctx.op("scalar", lambda e, a=a, bb2=bb2, kk=kk, bk_i=bk_i: e.activation(
                                    Pt[:, kk * 4096 + a: kk * 4096 + bb2],
                                    sc[:, bk_i * 512: bk_i * 512 + bb2 - a],
                                    AF.Exp), wait=[v_t] + ([pt_used[kk]] if pt_used[kk] is not None else []))
                                sc_used[bk_i] = s_t
                                bank_used[sb_i] = (s_t, v_t)
                            else:
                                s_t = ctx.op("scalar", lambda e, a=a, bb2=bb2, kk=kk, sb_i=sb_i: e.activation(
                                    Pt[:, kk * 4096 + a: kk * 4096 + bb2], b[sb_i][:, 0:bb2 - a],
                                    AF.Exp), wait=[sc_mm] + ([pt_used[kk]] if pt_used[kk] is not None else []))
                                bank_used[sb_i] = (s_t, None)
                exp_t = ctx.tok("scalar")
                if subp < 30:
                    continue
                for ch in range(8):
                    c0 = 512 * ch
                    for kk in range(2):
                        kt = 2 * kg + kk
                        blk = h01 * 32 + kt
                        pvw = []
                        if kk == 0:
                            pvw = [exp_t, vtok_t]
                            if ctx_acc_t is not None:
                                pvw.append(ctx_acc_t)
                        ctx.op("tensor", lambda e, blk=blk, kk=kk, c0=c0: e.matmul(
                            b[5][0:65, :], vtok[:, blk * 65:(blk + 1) * 65],
                            Pt[:, kk * 4096 + c0: kk * 4096 + c0 + 512],
                            start=(kk == 0), stop=(kk == 1)),
                            wait=pvw, inc=([mm.inc()] if kk == 1 else []))
                    pv_mm = mm.tgt()
                    if first_group:
                        ctx_acc_t = ctx.op("vector", lambda e, c0=c0: e.tensor_copy(
                            ctxA[0:65, c0:c0 + 512], b[5][0:65, :]), wait=[pv_mm])
                    else:
                        ctx_acc_t = ctx.op("vector", lambda e, c0=c0: e.tensor_tensor(
                            ctxA[0:65, c0:c0 + 512], ctxA[0:65, c0:c0 + 512], b[5][0:65, :],
                            op=ALU.add), wait=[pv_mm])
                pt_used[0] = pt_used[1] = ctx_acc_t
                first_group = False
            if phase < 4 or subp < 30:
                continue
            fin_ctx = ctxA
            ev_prev = None
            for ch in range(8):
                c0 = 512 * ch
                rec_t = ctx.op("vector", lambda e, c0=c0, fin_ctx=fin_ctx: e.reciprocal(
                    recs[:, :], fin_ctx[64:65, c0:c0 + 512]), wait=[ctx.tok("vector")])
                bw = [rec_t, const_t]
                if ev_prev is not None:
                    bw.append(ev_prev)
                else:
                    bw.append(stv.at(stv.n))
                ctx.op("tensor", lambda e: e.matmul(
                    b[4][0:64, :], o128[0:1, 0:64], recs[0:1, :],
                    start=True, stop=True), wait=bw, inc=[mm.inc()])
                bc_mm = mm.tgt()
                ev_prev = ctx.op("vector", lambda e, c0=c0, fin_ctx=fin_ctx: e.tensor_tensor(
                    ctxev[:, :], fin_ctx[0:64, c0:c0 + 512], b[4][0:64, :], op=ALU.mult),
                    wait=[bc_mm] + ([stv.at(stv.n)] if ch > 0 else []))
                ctx.op("sync", lambda e, h=h, c0=c0: e.dma_start(
                    out=ctxT_all.rearrange("(hh p) t -> hh p t", p=64)[h, :, c0:c0 + 512],
                    in_=ctxev[:, :]), wait=[ev_prev], inc=[stv.inc(16)])

    ctxld = Pt
    wo_r = io["l0_woT"].rearrange("(ct p) f -> p ct f", p=128)
    ctx_r = ctxT_all.rearrange("(ct p) t -> p ct t", p=128)
    s0_r = s0_dram.rearrange("(dt p) t -> p dt t", p=128)
    for dt in range(8):
        dbw = dt % 2
        ctx.op("gpsimd", lambda e, dt=dt, dbw=dbw: e.dma_start(
            out=bass.AP(wslab, dbw * 1024, [[2048, 128], [128, 8], [1, 128]]),
            in_=wo_r[:, :, 128 * dt:128 * (dt + 1)]),
            wait=[mm.at(mm.n)], inc=[gld.inc(16)])
        w_t = gld.tgt()
        slab_done = {}
        for tk in range(8):
            bk_i = tk % 2
            dbs = tk % 2
            fw = [w_t, ctx.tok("vector"), ctx.tok("scalar"), stv.at(stv.n)]
            sw = [slab_done[tk - 2]] if tk >= 2 else [mm.at(mm.n)]
            ctx.op("sync", lambda e, tk=tk, dbs=dbs: e.dma_start(
                out=bass.AP(ctxld, dbs * 4096, [[8192, 128], [512, 8], [1, 512]]),
                in_=ctx_r[:, :, tk * 512:(tk + 1) * 512]),
                wait=sw, inc=[ld.inc(16)])
            c_t = ld.tgt()
            for ci in range(8):
                ctx.op("tensor", lambda e, dbw=dbw, ci=ci, dbs=dbs, bk_i=bk_i: e.matmul(
                    b[bk_i][:, :], wslab[:, dbw * 1024 + ci * 128:dbw * 1024 + (ci + 1) * 128],
                    ctxld[:, dbs * 4096 + ci * 512: dbs * 4096 + (ci + 1) * 512],
                    start=(ci == 0), stop=(ci == 7)),
                    wait=([c_t] + fw if ci == 0 else []),
                    inc=([mm.inc()] if ci == 7 else []))
            o_mm = mm.tgt()
            slab_done[tk] = o_mm
            v_t = ctx.op("vector", lambda e, dt=dt, bk_i=bk_i: e.tensor_scalar(
                s0ev[:, :], b[bk_i][:, :], boc[:, dt:dt + 1], None, op0=ALU.add),
                wait=[o_mm, stv.at(stv.n)])
            ctx.op("sync", lambda e, dt=dt, tk=tk: e.dma_start(
                out=s0_r[:, dt, tk * 512:(tk + 1) * 512], in_=s0ev[:, :]),
                wait=[v_t], inc=[stv.inc(16)])
    ctx.wait("sync", [stv.tgt()])
    return stv



def emit_l0moe(ctx, io, xn0_dram, ge_dram, s0_dram, xown_dram, x1_dram, tag="m0"):
    ld = ctx.sem(tag + "_ld")
    gld = ctx.sem(tag + "_gld")
    mm = ctx.sem(tag + "_mm")
    stv = ctx.sem(tag + "_st")

    xn0 = ctx.sb(tag + "_xn0", (128, 8 * 4096), BF16)
    wslab = ctx.sb(tag + "_wslab", (128, 2 * 1024), BF16)
    gebuf = ctx.sb(tag + "_gebuf", (1, 4096), F32)
    gebc = ctx.sb(tag + "_gebc", (128, 2 * 4096), BF16)
    mid = ctx.sb(tag + "_mid", (128, 4096), BF16)
    mg = ctx.sb(tag + "_mg", (128, 2 * 4096), BF16)
    bup = ctx.sb(tag + "_bup", (128, 4 * NF0), F32)
    bdn = ctx.sb(tag + "_bdn", (1, 8 * 1024), F32)
    o128 = ctx.sb(tag + "_o128", (1, 128), F32)
    mgld = ctx.sb(tag + "_mgld", (128, 2 * 512), BF16)
    g0sb = ctx.sb(tag + "_g0", (128, 512), BF16)
    s0sb = ctx.sb(tag + "_s0", (128, 2 * 512), F32)
    xo32 = ctx.sb(tag + "_xo", (128, 2 * 512), F32)
    t1sb = ctx.sb(tag + "_t1", (128, 512), F32)
    x1sb = ctx.sb(tag + "_x1", (128, 512), F32)

    midg_arr = ctx.dram(tag + "_midg", (8 * F0P, T), BF16).ap()

    b = [ctx.bank(i) for i in range(8)]

    ctx.op("sync", lambda e: e.dma_start(out=o128[:], in_=io["c_ones1x128_f32"][:, :]), inc=[ld.inc(16)])
    ctx.op("sync", lambda e: e.dma_start(
        out=bass.AP(bup, 0, [[4 * NF0, 128], [NF0, 4], [1, NF0]]),
        in_=io["l0_bup"].rearrange("g p f -> p g f")), inc=[ld.inc(16)])
    ctx.op("sync", lambda e: e.dma_start(out=bdn[:], in_=io["l0_bdn"][:, :]), inc=[ld.inc(16)])
    xn_r = xn0_dram.rearrange("(dt p) t -> p dt t", p=128)
    ctx.op("sync", lambda e: e.dma_start(
        out=bass.AP(xn0, 0, [[32768, 128], [4096, 8], [1, 4096]]), in_=xn_r[:, :, :]),
        inc=[ld.inc(16)])
    const_t = ld.tgt()

    wup_r = io["l0_wup"].rearrange("g (ct p) f -> g p ct f", p=128)
    for g in range(4):
        for j, e_i in enumerate((2 * g, 2 * g + 1)):
            ctx.op("sync", lambda e, e_i=e_i: e.dma_start(out=gebuf[:], in_=ge_dram[e_i:e_i + 1, :]),
                   wait=[mm.at(mm.n)], inc=[ld.inc(16)])
            gr_t = ld.tgt()
            for tk in range(8):
                ctx.op("tensor", lambda e, tk=tk: e.matmul(
                    b[6][:, :], o128[0:1, :], gebuf[0:1, tk * 512:(tk + 1) * 512],
                    start=True, stop=True),
                    wait=[const_t, gr_t, ctx.tok("vector")], inc=[mm.inc()])
                g_mm = mm.tgt()
                ctx.op("vector", lambda e, j=j, tk=tk: e.tensor_copy(
                    gebc[:, j * 4096 + tk * 512: j * 4096 + (tk + 1) * 512], b[6][:, :]),
                    wait=[g_mm])
        gebc_t = ctx.tok("vector")
        for ft in range(NF0):
            dbw = ft % 2
            ctx.op("gpsimd", lambda e, g=g, ft=ft, dbw=dbw: e.dma_start(
                out=bass.AP(wslab, dbw * 1024, [[2048, 128], [128, 8], [1, 128]]),
                in_=wup_r[g, :, :, 128 * ft:128 * (ft + 1)]),
                wait=[mm.at(mm.n)], inc=[gld.inc(16)])
            w_t = gld.tgt()
            for tk in range(8):
                bk_i = tk % 2
                fw = [w_t, const_t, ctx.tok("scalar"), ctx.tok("vector")]
                for ci in range(8):
                    ctx.op("tensor", lambda e, dbw=dbw, ci=ci, tk=tk, bk_i=bk_i: e.matmul(
                        b[bk_i][:, :], wslab[:, dbw * 1024 + ci * 128:dbw * 1024 + (ci + 1) * 128],
                        xn0[:, ci * 4096 + tk * 512: ci * 4096 + (tk + 1) * 512],
                        start=(ci == 0), stop=(ci == 7)),
                        wait=(fw if ci == 0 else []),
                        inc=([mm.inc()] if ci == 7 else []))
                u_mm = mm.tgt()
                ctx.op("scalar", lambda e, g=g, ft=ft, tk=tk, bk_i=bk_i: e.activation(
                    mid[:, tk * 512:(tk + 1) * 512], b[bk_i][:, :], AF.Silu,
                    bias=bup[:, g * NF0 + ft: g * NF0 + ft + 1], scale=1.0),
                    wait=[u_mm] + ([ctx.tok("vector")] if tk == 0 else []))
            mid_t = ctx.tok("scalar")
            for j in range(2):
                ctx.op("vector", lambda e, j=j: e.tensor_tensor(
                    mg[:, j * 4096:(j + 1) * 4096], mid[:, :], gebc[:, j * 4096:(j + 1) * 4096],
                    op=ALU.mult), wait=([mid_t, gebc_t, stv.at(stv.n)] if j == 0 else []))
                mg_t = ctx.tok("vector")
                e_i = 2 * g + j
                ctx.op("sync", lambda e, e_i=e_i, ft=ft, j=j: e.dma_start(
                    out=midg_arr[e_i * F0P + 128 * ft: e_i * F0P + 128 * (ft + 1), :],
                    in_=mg[:, j * 4096:(j + 1) * 4096]), wait=[mg_t], inc=[stv.inc(16)])

    wdn_r = io["l0_wdn"].rearrange("g (ct p) f -> g p ct f", p=128)
    s0_r = s0_dram.rearrange("(dt p) t -> p dt t", p=128)
    x1_r = x1_dram.rearrange("(dt p) t -> p dt t", p=128)
    xo_r = xown_dram.rearrange("(dt p) t -> p dt t", p=128)
    mida_r = midg_arr.rearrange("(r p) t -> p r t", p=128)
    for tk in range(8):
        nacc = 0
        acc_done = {}
        for e_i in range(8):
            ctx.op("sync", lambda e, e_i=e_i: e.dma_start(out=gebuf[:], in_=ge_dram[e_i:e_i + 1, :]),
                   wait=[mm.at(mm.n)], inc=[ld.inc(16)])
            gr_t = ld.tgt()
            for dt in range(8):
                ctx.op("tensor", lambda e, e_i=e_i, dt=dt, tk=tk, nacc=nacc: e.matmul(
                    b[dt][:, :], bdn[0:1, e_i * 1024 + dt * 128: e_i * 1024 + (dt + 1) * 128],
                    gebuf[0:1, tk * 512:(tk + 1) * 512],
                    start=(nacc == 0), stop=False),
                    wait=([gr_t] + ([const_t, ctx.tok("scalar"), ctx.tok("vector"), stv.at(stv.n)] if nacc == 0 else []) if dt == 0 else []))
            nacc += 1
            for fc in range(NF0):
                kk2 = e_i * NF0 + fc
                dbw = kk2 % 2
                lw = [acc_done[kk2 - 2]] if kk2 >= 2 else [mm.at(mm.n)]
                ctx.op("gpsimd", lambda e, e_i=e_i, fc=fc, dbw=dbw: e.dma_start(
                    out=bass.AP(wslab, dbw * 1024, [[2048, 128], [128, 8], [1, 128]]),
                    in_=wdn_r[e_i, :, fc, :]),
                    wait=lw, inc=[gld.inc(16)])
                ctx.op("sync", lambda e, e_i=e_i, fc=fc, tk=tk, dbw=dbw: e.dma_start(
                    out=mgld[:, dbw * 512:(dbw + 1) * 512],
                    in_=mida_r[:, e_i * NF0 + fc, tk * 512:(tk + 1) * 512]),
                    wait=lw, inc=[ld.inc(16)])
                wm_t = gld.tgt()
                mi_t = ld.tgt()
                last = (e_i == 7 and fc == NF0 - 1)
                for dt in range(8):
                    ctx.op("tensor", lambda e, dbw=dbw, dt=dt, last=last: e.matmul(
                        b[dt][:, :], wslab[:, dbw * 1024 + dt * 128:dbw * 1024 + (dt + 1) * 128],
                        mgld[:, dbw * 512:(dbw + 1) * 512],
                        start=False, stop=last),
                        wait=([wm_t, mi_t] if dt == 0 else []),
                        inc=([mm.inc()] if dt == 7 else []))
                acc_done[kk2] = mm.tgt()
        y_mm = mm.tgt()
        for dt in range(8):
            dbs = dt % 2
            ctx.op("sync", lambda e, dt=dt, tk=tk, dbs=dbs: e.dma_start(
                out=s0sb[:, dbs * 512:(dbs + 1) * 512], in_=s0_r[:, dt, tk * 512:(tk + 1) * 512]),
                wait=[ctx.tok("vector")], inc=[ld.inc(16)])
            ctx.op("sync", lambda e, dt=dt, tk=tk, dbs=dbs: e.dma_start(
                out=xo32[:, dbs * 512:(dbs + 1) * 512], in_=xo_r[:, dt, tk * 512:(tk + 1) * 512]),
                inc=[ld.inc(16)])
            sx_t = ld.tgt()
            ctx.op("scalar", lambda e, dt=dt: e.activation(
                g0sb[:, :], b[dt][:, :], AF.Silu), wait=[y_mm])
            g0_t = ctx.tok("scalar")
            ctx.op("vector", lambda e, dbs=dbs: e.tensor_tensor(
                t1sb[:, :], s0sb[:, dbs * 512:(dbs + 1) * 512], g0sb[:, :], op=ALU.mult),
                wait=[g0_t, sx_t])
            x1_t = ctx.op("vector", lambda e, dbs=dbs: e.tensor_tensor(
                x1sb[:, :], t1sb[:, :], xo32[:, dbs * 512:(dbs + 1) * 512], op=ALU.add))
            ctx.op("sync", lambda e, dt=dt, tk=tk: e.dma_start(
                out=x1_r[:, dt, tk * 512:(tk + 1) * 512], in_=x1sb[:, :]),
                wait=[x1_t], inc=[stv.inc(16)])
            ctx.wait("scalar", [stv.tgt()])
            ctx.wait("vector", [stv.tgt()])
    ctx.wait("sync", [stv.tgt()])
    return stv



def emit_linear(ctx, io, wT, src_bf, n_in, n_out, dst, dst_dt, bias_cols=None,
                act=None, resid32=None, scale=1.0, extra_wait=(), tag="lin"):
    ld = ctx.sem(tag + "_ld")
    gld = ctx.sem(tag + "_gld")
    mm = ctx.sem(tag + "_mm")
    stv = ctx.sem(tag + "_st")
    nci = n_in // 128
    nfo = n_out // 128

    src_sb = ctx.sb(tag + "_src", (128, nci * 4096), BF16)
    wslab = ctx.sb(tag + "_w", (128, 2 * nci * 128), BF16)
    bcol = ctx.sb(tag + "_b", (128, nfo), F32) if bias_cols is not None else None
    ev32 = ctx.sb(tag + "_ev32", (128, 512), F32)
    evb = ctx.sb(tag + "_evb", (128, 512), dst_dt)
    res = ctx.sb(tag + "_res", (128, 2 * 512), F32) if resid32 is not None else None

    b = [ctx.bank(i) for i in range(8)]

    src_r = src_bf.rearrange("(ct p) t -> p ct t", p=128)
    ctx.op("sync", lambda e: e.dma_start(
        out=bass.AP(src_sb, 0, [[nci * 4096, 128], [4096, nci], [1, 4096]]), in_=src_r[:, :, :]),
        inc=[ld.inc(16)])
    if bias_cols is not None:
        ctx.op("sync", lambda e: e.dma_start(out=bcol[:], in_=bias_cols[:, :]), inc=[ld.inc(16)])
    const_t = ld.tgt()

    w_r = wT.rearrange("(ct p) f -> p ct f", p=128)
    dst_r = dst.rearrange("(dt p) t -> p dt t", p=128)
    res_r = resid32.rearrange("(dt p) t -> p dt t", p=128) if resid32 is not None else None
    for dt in range(nfo):
        dbw = dt % 2
        ctx.op("gpsimd", lambda e, dt=dt, dbw=dbw: e.dma_start(
            out=bass.AP(wslab, dbw * nci * 128, [[2 * nci * 128, 128], [128, nci], [1, 128]]),
            in_=w_r[:, :, 128 * dt:128 * (dt + 1)]),
            wait=[mm.at(mm.n)] + (list(extra_wait) if dt <= 1 else []), inc=[gld.inc(16)])
        w_t = gld.tgt()
        for tk in range(8):
            bk_i = tk % 2
            fw = [w_t, const_t, ctx.tok("scalar"), ctx.tok("vector")]
            for ci in range(nci):
                ctx.op("tensor", lambda e, dbw=dbw, ci=ci, tk=tk, bk_i=bk_i: e.matmul(
                    b[bk_i][:, :], wslab[:, dbw * nci * 128 + ci * 128:dbw * nci * 128 + (ci + 1) * 128],
                    src_sb[:, ci * 4096 + tk * 512: ci * 4096 + (tk + 1) * 512],
                    start=(ci == 0), stop=(ci == nci - 1)),
                    wait=(fw if ci == 0 else []),
                    inc=([mm.inc()] if ci == nci - 1 else []))
            o_mm = mm.tgt()
            if resid32 is not None:
                dbs = tk % 2
                ctx.op("sync", lambda e, dt=dt, tk=tk, dbs=dbs: e.dma_start(
                    out=res[:, dbs * 512:(dbs + 1) * 512], in_=res_r[:, dt, tk * 512:(tk + 1) * 512]),
                    wait=[ctx.tok("vector")], inc=[ld.inc(16)])
                r_t = ld.tgt()
            af = act if act is not None else AF.Identity
            biasap = bcol[:, dt:dt + 1] if bias_cols is not None else 0
            if resid32 is None:
                tok = ctx.op("scalar", lambda e, dt=dt, bk_i=bk_i, af=af, biasap=biasap: e.activation(
                    evb[:, :], b[bk_i][:, :], af,
                    **({"bias": biasap} if bias_cols is not None else {}), scale=scale),
                    wait=[o_mm, stv.at(stv.n)])
                ctx.op("sync", lambda e, dt=dt, tk=tk: e.dma_start(
                    out=dst_r[:, dt, tk * 512:(tk + 1) * 512], in_=evb[:, :]),
                    wait=[tok], inc=[stv.inc(16)])
            else:
                tok = ctx.op("scalar", lambda e, dt=dt, bk_i=bk_i, af=af, biasap=biasap: e.activation(
                    ev32[:, :], b[bk_i][:, :], af,
                    **({"bias": biasap} if bias_cols is not None else {}), scale=scale),
                    wait=[o_mm, ctx.tok("vector")])
                v_t = ctx.op("vector", lambda e, dbs=dbs: e.tensor_tensor(
                    evb[:, :], ev32[:, :], res[:, dbs * 512:(dbs + 1) * 512], op=ALU.add),
                    wait=[tok, r_t, stv.at(stv.n)])
                ctx.op("sync", lambda e, dt=dt, tk=tk: e.dma_start(
                    out=dst_r[:, dt, tk * 512:(tk + 1) * 512], in_=evb[:, :]),
                    wait=[v_t], inc=[stv.inc(16)])
    ctx.wait("sync", [stv.tgt()])
    return stv



def emit_l1attn_core(ctx, io, t2_dram, ares_dram, a2_dram, tag="a1"):
    ld = ctx.sem(tag + "_ld")
    gld = ctx.sem(tag + "_gld")
    mm = ctx.sem(tag + "_mm")
    stv = ctx.sem(tag + "_st")

    xn = ctx.sb(tag + "_xn", (128, 8 * 4096), BF16)
    wslab = ctx.sb(tag + "_wslab", (128, 2 * 1024), BF16)
    QT = ctx.sb(tag + "_QT", (128, 4096), BF16)
    KT = ctx.sb(tag + "_KT", (128, 4096), BF16)
    VT = ctx.sb(tag + "_VT", (128, 4096), BF16)
    vtok = ctx.sb(tag + "_vtok", (128, 2 * 32 * 65), BF16)
    Pt = ctx.sb(tag + "_Pt", (128, 2 * 4096), BF16)
    sc = ctx.sb(tag + "_sc", (128, 2 * 512), F32)
    ctxA = ctx.sb(tag + "_ctxA", (65, 4096), F32)
    recs = ctx.sb(tag + "_recs", (1, 512), F32)
    sum_eps = ctx.sb(tag + "_se", (1, 512), F32)
    ctxev = ctx.sb(tag + "_ctxev", (64, 512), BF16)
    cmask = ctx.sb(tag + "_cmask", (128, 4 * 512), F32)
    id_bf = ctx.sb(tag + "_idbf", (128, 128), BF16)
    o128 = ctx.sb(tag + "_o128", (1, 128), F32)
    bqc = ctx.sb(tag + "_bqc", (128, 8), F32)
    bkc = ctx.sb(tag + "_bkc", (128, 8), F32)
    bvc = ctx.sb(tag + "_bvc", (128, 8), F32)
    boc = ctx.sb(tag + "_boc", (128, 8), F32)
    aev = ctx.sb(tag + "_aev", (128, 512), F32)
    ares = ctx.sb(tag + "_ares", (128, 2 * 512), F32)
    ctxld = Pt

    ctxT_all = ctx.dram(tag + "_ctxT", (D, T), BF16).ap()
    b = [ctx.bank(i) for i in range(8)]

    for nm, dst in [("c_id_bf", id_bf), ("c_ones1x128_f32", o128), ("l1_bq", bqc),
                    ("l1_bk", bkc), ("l1_bv", bvc), ("l1_bo", boc)]:
        ctx.op("sync", lambda e, nm=nm, dst=dst: e.dma_start(out=dst[:], in_=io[nm][:, :]), inc=[ld.inc(16)])
    ctx.op("sync", lambda e: e.dma_start(
        out=bass.AP(cmask, 0, [[2048, 128], [512, 4], [1, 512]]),
        in_=bass.AP(io["c_causal"].tensor, 0, [[512, 128], [65536, 4], [1, 512]])), inc=[ld.inc(16)])
    xn_r = t2_dram.rearrange("(dt p) t -> p dt t", p=128)
    ctx.op("sync", lambda e: e.dma_start(
        out=bass.AP(xn, 0, [[32768, 128], [4096, 8], [1, 4096]]), in_=xn_r[:, :, :]),
        inc=[ld.inc(16)])
    const_t = ld.tgt()

    ws = {"q": io["l1_wqT"], "k": io["l1_wkT"], "v": io["l1_wvT"]}
    evp = {"q": (SCALE_L1, bqc), "k": (1.0, bkc), "v": (1.0, bvc)}
    sbq_t = ctx.op("vector", lambda e: e.tensor_scalar(bqc[:], bqc[:], SCALE_L1, None, op0=ALU.mult),
                   wait=[const_t])

    for pair in range(8):
        proj_done = {}
        for mi, mat in enumerate(("q", "k", "v")):
            dbw = mi % 2
            ctx.op("gpsimd", lambda e, mat=mat, dbw=dbw, pair=pair: e.dma_start(
                out=bass.AP(wslab, dbw * 1024, [[2048, 128], [128, 8], [1, 128]]),
                in_=ws[mat].rearrange("(ct p) f -> p ct f", p=128)[:, :, 128 * pair:128 * (pair + 1)]),
                wait=[mm.at(mm.n)], inc=[gld.inc(16)])
            w_t = gld.tgt()
            scale, bias = evp[mat]
            dst = {"q": QT, "k": KT, "v": VT}[mat]
            for tk in range(8):
                bk_i = 4 + (tk % 2)
                fw = [w_t, const_t, sbq_t, ctx.tok("scalar"), ctx.tok("vector")]
                for ci in range(8):
                    ctx.op("tensor", lambda e, dbw=dbw, tk=tk, ci=ci, bk_i=bk_i: e.matmul(
                        b[bk_i][:, :], wslab[:, dbw * 1024 + ci * 128:dbw * 1024 + (ci + 1) * 128],
                        xn[:, ci * 4096 + tk * 512: ci * 4096 + (tk + 1) * 512],
                        start=(ci == 0), stop=(ci == 7)),
                        wait=(fw if ci == 0 else []),
                        inc=([mm.inc()] if ci == 7 else []))
                pj_mm = mm.tgt()
                ctx.op("scalar", lambda e, dst=dst, tk=tk, bk_i=bk_i, scale=scale, bias=bias, pair=pair: e.activation(
                    dst[:, tk * 512:(tk + 1) * 512], b[bk_i][:, :], AF.Identity,
                    bias=bias[:, pair:pair + 1], scale=scale), wait=[pj_mm])
            proj_done[mat] = ctx.tok("scalar")

        ctx.op("vector", lambda e: e.memset(
            bass.AP(vtok, 64, [[4160, 128], [65, 64]]), 1.0), wait=[proj_done["v"]])
        for h01 in range(2):
            for kt in range(32):
                blk = h01 * 32 + kt
                pbf = b[6][:, 0:32].bitcast(BF16)
                ctx.op("tensor", lambda e, h01=h01, kt=kt, pbf=pbf: e.transpose(
                    pbf, VT[64 * h01:64 * h01 + 64, 128 * kt:128 * (kt + 1)],
                    id_bf[64 * h01:64 * h01 + 64, 64 * h01:64 * h01 + 64]),
                    wait=[ctx.tok("vector"), const_t], inc=[mm.inc()])
                v_mm = mm.tgt()
                ctx.op("vector", lambda e, blk=blk, pbf=pbf: e.tensor_copy(
                    vtok[:, blk * 65: blk * 65 + 64], pbf), wait=[v_mm])
        vtok_t = ctx.tok("vector")

        for h01 in range(2):
            h = 2 * pair + h01
            pt_used = [None, None]
            bank_used = {i: (ctx.tok("scalar"), ctx.tok("vector")) for i in range(2)}
            sc_used = [ctx.tok("scalar"), ctx.tok("scalar")]
            ctx_acc_t = ctx.tok("vector")
            written = set()
            first_acc = {}
            for kg in range(16):
                for kk in range(2):
                    kt = 2 * kg + kk
                    k0 = 128 * kt
                    ch_lo = k0 // 512
                    for ch in range(ch_lo, 8):
                        c0 = 512 * ch
                        bk_i = ch % 2
                        bu = bank_used.get(bk_i)
                        waits = [proj_done["q"], proj_done["k"]] + [t for t in bu if t is not None]
                        ctx.op("tensor", lambda e, c0=c0, k0=k0, bk_i=bk_i, h01=h01: e.matmul(
                            b[bk_i][:, :], KT[64 * h01:64 * h01 + 64, k0:k0 + 128],
                            QT[64 * h01:64 * h01 + 64, c0:c0 + 512],
                            start=True, stop=True, tile_position=(64 * h01, 0)),
                            wait=waits, inc=[mm.inc()])
                        sc_mm = mm.tgt()
                        if ch == ch_lo:
                            v = (k0 - 512 * ch_lo) // 128
                            vw = [sc_mm]
                            if sc_used[bk_i] is not None:
                                vw.append(sc_used[bk_i])
                            v_t = ctx.op("vector", lambda e, v=v, bk_i=bk_i: e.tensor_tensor(
                                sc[:, bk_i * 512:(bk_i + 1) * 512], b[bk_i][:, :],
                                cmask[:, v * 512:(v + 1) * 512], op=ALU.mult),
                                wait=vw)
                            s_t = ctx.op("scalar", lambda e, c0=c0, kk=kk, bk_i=bk_i: e.activation(
                                Pt[:, kk * 4096 + c0: kk * 4096 + c0 + 512],
                                sc[:, bk_i * 512:(bk_i + 1) * 512], AF.Relu),
                                wait=[v_t] + ([pt_used[kk]] if pt_used[kk] is not None else []))
                            sc_used[bk_i] = s_t
                            bank_used[bk_i] = (s_t, v_t)
                        else:
                            s_t = ctx.op("scalar", lambda e, c0=c0, kk=kk, bk_i=bk_i: e.activation(
                                Pt[:, kk * 4096 + c0: kk * 4096 + c0 + 512], b[bk_i][:, :],
                                AF.Relu), wait=[sc_mm] + ([pt_used[kk]] if pt_used[kk] is not None else []))
                            bank_used[bk_i] = (s_t, None)
                        written.add((kk, ch))
                exp_t = ctx.tok("scalar")
                for ch in range(8):
                    c0 = 512 * ch
                    valid = [kk for kk in range(2) if (kk, ch) in written]
                    if not valid:
                        continue
                    nv = len(valid)
                    for vi, kk in enumerate(valid):
                        kt = 2 * kg + kk
                        blk = h01 * 32 + kt
                        pvw = []
                        if vi == 0:
                            pvw = [exp_t, vtok_t, ctx_acc_t]
                        ctx.op("tensor", lambda e, blk=blk, kk=kk, c0=c0, vi=vi, nv=nv: e.matmul(
                            b[5][0:65, :], vtok[:, blk * 65:(blk + 1) * 65],
                            Pt[:, kk * 4096 + c0: kk * 4096 + c0 + 512],
                            start=(vi == 0), stop=(vi == nv - 1)),
                            wait=pvw, inc=([mm.inc()] if vi == nv - 1 else []))
                    pv_mm = mm.tgt()
                    if not first_acc.get(ch):
                        ctx_acc_t = ctx.op("vector", lambda e, c0=c0: e.tensor_copy(
                            ctxA[0:65, c0:c0 + 512], b[5][0:65, :]), wait=[pv_mm])
                        first_acc[ch] = True
                    else:
                        ctx_acc_t = ctx.op("vector", lambda e, c0=c0: e.tensor_tensor(
                            ctxA[0:65, c0:c0 + 512], ctxA[0:65, c0:c0 + 512], b[5][0:65, :],
                            op=ALU.add), wait=[pv_mm])
                pt_used[0] = pt_used[1] = ctx_acc_t
                written = set()
            ev_prev = None
            for ch in range(8):
                c0 = 512 * ch
                ctx.op("vector", lambda e, c0=c0: e.tensor_scalar(
                    sum_eps[:, :], ctxA[64:65, c0:c0 + 512], 1e-9, None, op0=ALU.add),
                    wait=[ctx.tok("vector")])
                rec_t = ctx.op("vector", lambda e: e.reciprocal(recs[:, :], sum_eps[:, :]))
                bw = [rec_t, const_t]
                if ev_prev is not None:
                    bw.append(ev_prev)
                else:
                    bw.append(stv.at(stv.n))
                ctx.op("tensor", lambda e: e.matmul(
                    b[4][0:64, :], o128[0:1, 0:64], recs[0:1, :],
                    start=True, stop=True), wait=bw, inc=[mm.inc()])
                bc_mm = mm.tgt()
                ev_prev = ctx.op("vector", lambda e, c0=c0: e.tensor_tensor(
                    ctxev[:, :], ctxA[0:64, c0:c0 + 512], b[4][0:64, :], op=ALU.mult),
                    wait=[bc_mm] + ([stv.at(stv.n)] if ch > 0 else []))
                ctx.op("sync", lambda e, h=h, c0=c0: e.dma_start(
                    out=ctxT_all.rearrange("(hh p) t -> hh p t", p=64)[h, :, c0:c0 + 512],
                    in_=ctxev[:, :]), wait=[ev_prev], inc=[stv.inc(16)])

    wo_r = io["l1_woT"].rearrange("(ct p) f -> p ct f", p=128)
    ctx_r = ctxT_all.rearrange("(ct p) t -> p ct t", p=128)
    a2_r = a2_dram.rearrange("(dt p) t -> p dt t", p=128)
    ares_r = ares_dram.rearrange("(dt p) t -> p dt t", p=128)
    for dt in range(8):
        dbw = dt % 2
        ctx.op("gpsimd", lambda e, dt=dt, dbw=dbw: e.dma_start(
            out=bass.AP(wslab, dbw * 1024, [[2048, 128], [128, 8], [1, 128]]),
            in_=wo_r[:, :, 128 * dt:128 * (dt + 1)]),
            wait=[mm.at(mm.n)], inc=[gld.inc(16)])
        w_t = gld.tgt()
        slab_done = {}
        for tk in range(8):
            bk_i = tk % 2
            dbs = tk % 2
            sw = [slab_done[tk - 2]] if tk >= 2 else [mm.at(mm.n)]
            ctx.op("sync", lambda e, tk=tk, dbs=dbs: e.dma_start(
                out=bass.AP(ctxld, dbs * 4096, [[8192, 128], [512, 8], [1, 512]]),
                in_=ctx_r[:, :, tk * 512:(tk + 1) * 512]),
                wait=sw, inc=[ld.inc(16)])
            c_t = ld.tgt()
            ctx.op("sync", lambda e, dt=dt, tk=tk, dbs=dbs: e.dma_start(
                out=ares[:, dbs * 512:(dbs + 1) * 512], in_=ares_r[:, dt, tk * 512:(tk + 1) * 512]),
                wait=[ctx.tok("vector")], inc=[ld.inc(16)])
            r_t = ld.tgt()
            fw = [w_t, ctx.tok("vector"), ctx.tok("scalar"), stv.at(stv.n)]
            for ci in range(8):
                ctx.op("tensor", lambda e, dbw=dbw, ci=ci, dbs=dbs, bk_i=bk_i: e.matmul(
                    b[bk_i][:, :], wslab[:, dbw * 1024 + ci * 128:dbw * 1024 + (ci + 1) * 128],
                    ctxld[:, dbs * 4096 + ci * 512: dbs * 4096 + (ci + 1) * 512],
                    start=(ci == 0), stop=(ci == 7)),
                    wait=([c_t] + fw if ci == 0 else []),
                    inc=([mm.inc()] if ci == 7 else []))
            o_mm = mm.tgt()
            slab_done[tk] = o_mm
            s_t = ctx.op("scalar", lambda e, dt=dt, bk_i=bk_i: e.activation(
                aev[:, :], b[bk_i][:, :], AF.Identity, bias=boc[:, dt:dt + 1], scale=1.0),
                wait=[o_mm, ctx.tok("vector")])
            v_t = ctx.op("vector", lambda e, dbs=dbs: e.tensor_tensor(
                aev[:, :], aev[:, :], ares[:, dbs * 512:(dbs + 1) * 512], op=ALU.add),
                wait=[s_t, r_t, stv.at(stv.n)])
            ctx.op("sync", lambda e, dt=dt, tk=tk: e.dma_start(
                out=a2_r[:, dt, tk * 512:(tk + 1) * 512], in_=aev[:, :]),
                wait=[v_t], inc=[stv.inc(16)])
    ctx.wait("sync", [stv.tgt()])
    return stv



def emit_l1ff(ctx, io, t3_dram, a2_dram, s1_dram, tag="f1"):
    ld = ctx.sem(tag + "_ld")
    gld = ctx.sem(tag + "_gld")
    mm = ctx.sem(tag + "_mm")
    stv = ctx.sem(tag + "_st")

    xn = ctx.sb(tag + "_xn", (128, 8 * 4096), BF16)
    wslab = ctx.sb(tag + "_wslab", (128, 2 * 1024), BF16)
    h1sb = ctx.sb(tag + "_h1", (128, 4096), BF16)
    hld = ctx.sb(tag + "_hld", (128, 2 * 512), BF16)
    f1b = ctx.sb(tag + "_f1b", (128, 32), F32)
    f2b = ctx.sb(tag + "_f2b", (128, 8), F32)
    a2sb = ctx.sb(tag + "_a2", (128, 2 * 512), F32)
    ev32 = ctx.sb(tag + "_ev32", (128, 512), F32)
    s1sb = ctx.sb(tag + "_s1", (128, 512), F32)

    h1_arr = ctx.dram(tag + "_h1a", (FF, T), BF16).ap()
    b = [ctx.bank(i) for i in range(8)]

    ctx.op("sync", lambda e: e.dma_start(out=f1b[:], in_=io["l1_ff1b"][:, :]), inc=[ld.inc(16)])
    ctx.op("sync", lambda e: e.dma_start(out=f2b[:], in_=io["l1_ff2b"][:, :]), inc=[ld.inc(16)])
    xn_r = t3_dram.rearrange("(dt p) t -> p dt t", p=128)
    ctx.op("sync", lambda e: e.dma_start(
        out=bass.AP(xn, 0, [[32768, 128], [4096, 8], [1, 4096]]), in_=xn_r[:, :, :]),
        inc=[ld.inc(16)])
    const_t = ld.tgt()

    w1_r = io["l1_ff1T"].rearrange("(ct p) f -> p ct f", p=128)
    for ft in range(32):
        dbw = ft % 2
        ctx.op("gpsimd", lambda e, ft=ft, dbw=dbw: e.dma_start(
            out=bass.AP(wslab, dbw * 1024, [[2048, 128], [128, 8], [1, 128]]),
            in_=w1_r[:, :, 128 * ft:128 * (ft + 1)]),
            wait=[mm.at(mm.n)], inc=[gld.inc(16)])
        w_t = gld.tgt()
        for tk in range(8):
            bk_i = tk % 2
            fw = [w_t, const_t, ctx.tok("scalar"), stv.at(stv.n)]
            for ci in range(8):
                ctx.op("tensor", lambda e, dbw=dbw, ci=ci, tk=tk, bk_i=bk_i: e.matmul(
                    b[bk_i][:, :], wslab[:, dbw * 1024 + ci * 128:dbw * 1024 + (ci + 1) * 128],
                    xn[:, ci * 4096 + tk * 512: ci * 4096 + (tk + 1) * 512],
                    start=(ci == 0), stop=(ci == 7)),
                    wait=(fw if ci == 0 else []),
                    inc=([mm.inc()] if ci == 7 else []))
            u_mm = mm.tgt()
            ctx.op("scalar", lambda e, ft=ft, tk=tk, bk_i=bk_i: e.activation(
                h1sb[:, tk * 512:(tk + 1) * 512], b[bk_i][:, :], AF.Relu,
                bias=f1b[:, ft:ft + 1], scale=1.0),
                wait=[u_mm] + ([stv.at(stv.n)] if tk == 0 else []))
        h_t = ctx.tok("scalar")
        ctx.op("sync", lambda e, ft=ft: e.dma_start(
            out=h1_arr[128 * ft:128 * (ft + 1), :], in_=h1sb[:, :]), wait=[h_t], inc=[stv.inc(16)])

    w2_r = io["l1_ff2T"].rearrange("(ct p) f -> p ct f", p=128)
    h_r = h1_arr.rearrange("(ct p) t -> p ct t", p=128)
    s1_r = s1_dram.rearrange("(dt p) t -> p dt t", p=128)
    a2_r = a2_dram.rearrange("(dt p) t -> p dt t", p=128)
    for tk in range(8):
        fc_done = {}
        for fc in range(32):
            dbw = fc % 2
            lw = [fc_done[fc - 2]] if fc >= 2 else [mm.at(mm.n)]
            ctx.op("gpsimd", lambda e, fc=fc, dbw=dbw: e.dma_start(
                out=bass.AP(wslab, dbw * 1024, [[2048, 128], [128, 8], [1, 128]]),
                in_=w2_r[:, fc, :].rearrange("p (dd f) -> p dd f", f=128)),
                wait=lw, inc=[gld.inc(16)])
            ctx.op("sync", lambda e, fc=fc, tk=tk, dbw=dbw: e.dma_start(
                out=hld[:, dbw * 512:(dbw + 1) * 512], in_=h_r[:, fc, tk * 512:(tk + 1) * 512]),
                wait=lw + [stv.at(stv.n)], inc=[ld.inc(16)])
            w_t = gld.tgt()
            hh_t = ld.tgt()
            for dt in range(8):
                ctx.op("tensor", lambda e, dbw=dbw, dt=dt, fc=fc: e.matmul(
                    b[dt][:, :], wslab[:, dbw * 1024 + dt * 128:dbw * 1024 + (dt + 1) * 128],
                    hld[:, dbw * 512:(dbw + 1) * 512],
                    start=(fc == 0), stop=(fc == 31)),
                    wait=([w_t, hh_t] + ([ctx.tok("vector")] if (fc == 0 and dt == 0) else []) if dt == 0 else []),
                    inc=([mm.inc()] if dt == 7 else []))
            fc_done[fc] = mm.tgt()
        y_mm = mm.tgt()
        for dt in range(8):
            dbs = dt % 2
            ctx.op("sync", lambda e, dt=dt, tk=tk, dbs=dbs: e.dma_start(
                out=a2sb[:, dbs * 512:(dbs + 1) * 512], in_=a2_r[:, dt, tk * 512:(tk + 1) * 512]),
                wait=[ctx.tok("vector")], inc=[ld.inc(16)])
            a_t = ld.tgt()
            s_t = ctx.op("scalar", lambda e, dt=dt: e.activation(
                ev32[:, :], b[dt][:, :], AF.Identity, bias=f2b[:, dt:dt + 1], scale=1.0),
                wait=[y_mm, ctx.tok("vector")])
            v_t = ctx.op("vector", lambda e, dbs=dbs: e.tensor_tensor(
                s1sb[:, :], ev32[:, :], a2sb[:, dbs * 512:(dbs + 1) * 512], op=ALU.add),
                wait=[s_t, a_t, stv.at(stv.n)])
            ctx.op("sync", lambda e, dt=dt, tk=tk: e.dma_start(
                out=s1_r[:, dt, tk * 512:(tk + 1) * 512], in_=s1sb[:, :]),
                wait=[v_t], inc=[stv.inc(16)])
    ctx.wait("sync", [stv.tgt()])
    return stv



def emit_l1moe(ctx, io, xn1_dram, s1_dram, x1_dram, out_dram, tag="m1"):
    ld = ctx.sem(tag + "_ld")
    gld = ctx.sem(tag + "_gld")
    mm = ctx.sem(tag + "_mm")
    stv = ctx.sem(tag + "_st")

    xn = ctx.sb(tag + "_xn", (128, 8 * 4096), BF16)
    wslab = ctx.sb(tag + "_wslab", (128, 2 * 1024), BF16)
    rw = ctx.sb(tag + "_rw", (128, 8 * 8), BF16)
    idf = ctx.sb(tag + "_idf", (128, 128), F32)
    o128 = ctx.sb(tag + "_o128", (1, 128), F32)
    lg = ctx.sb(tag + "_lg", (8, 512), F32)
    ltok = ctx.sb(tag + "_ltok", (128, 256), F32)
    v1 = ctx.sb(tag + "_v1", (128, 32), F32)
    v2 = ctx.sb(tag + "_v2", (128, 32), F32)
    mask1 = ctx.sb(tag + "_mk1", (128, 256), F32)
    mask2 = ctx.sb(tag + "_mk2", (128, 256), F32)
    l2 = ctx.sb(tag + "_l2", (128, 256), F32)
    m9 = ctx.sb(tag + "_m9", (128, 256), F32)
    g1s = ctx.sb(tag + "_g1s", (128, 32), F32)
    g2s = ctx.sb(tag + "_g2s", (128, 32), F32)
    den = ctx.sb(tag + "_den", (128, 32), F32)
    rec = ctx.sb(tag + "_rec", (128, 32), F32)
    gn1 = ctx.sb(tag + "_gn1", (128, 32), F32)
    gn2 = ctx.sb(tag + "_gn2", (128, 32), F32)
    getok = ctx.sb(tag + "_getok", (128, 256), F32)
    getr = ctx.sb(tag + "_getr", (32, 128), F32)
    geR = ctx.sb(tag + "_geR", (1, 4096), F32)
    gebc = ctx.sb(tag + "_gebc", (128, 4096), BF16)
    sg = ctx.sb(tag + "_sg", (128, 4096), BF16)
    ug = ctx.sb(tag + "_ug", (128, 4096), BF16)
    hsb = ctx.sb(tag + "_h", (128, 4096), BF16)
    hld = ctx.sb(tag + "_hld", (128, 2 * 512), BF16)
    s1sb = ctx.sb(tag + "_s1", (128, 2 * 512), F32)
    x1sb = ctx.sb(tag + "_x1", (128, 2 * 512), F32)
    g1v = ctx.sb(tag + "_g1v", (128, 512), BF16)
    t1sb = ctx.sb(tag + "_t1", (128, 512), F32)
    osb = ctx.sb(tag + "_o", (128, 512), F32)

    ge1_dram = ctx.dram(tag + "_ge1", (8, T), F32).ap()
    h_arr = ctx.dram(tag + "_ha", (8 * H1, T), BF16).ap()
    b = [ctx.bank(i) for i in range(8)]

    ctx.op("sync", lambda e: e.dma_start(out=idf[:], in_=io["c_id_f32"][:, :]), inc=[ld.inc(16)])
    ctx.op("sync", lambda e: e.dma_start(out=o128[:], in_=io["c_ones1x128_f32"][:, :]), inc=[ld.inc(16)])
    ctx.op("gpsimd", lambda e: e.dma_start(
        out=bass.AP(rw, 0, [[64, 128], [8, 8], [1, 8]]),
        in_=io["l1_rwT"].rearrange("(ct p) e -> p ct e", p=128)), inc=[gld.inc(16)])
    xn_r = xn1_dram.rearrange("(dt p) t -> p dt t", p=128)
    ctx.op("sync", lambda e: e.dma_start(
        out=bass.AP(xn, 0, [[32768, 128], [4096, 8], [1, 4096]]), in_=xn_r[:, :, :]),
        inc=[ld.inc(16)])
    const_t = ld.tgt()
    rw_t = gld.tgt()

    for tk in range(8):
        for ci in range(8):
            ctx.op("tensor", lambda e, ci=ci, tk=tk: e.matmul(
                b[0][0:8, :], rw[:, ci * 8:(ci + 1) * 8],
                xn[:, ci * 4096 + tk * 512: ci * 4096 + (tk + 1) * 512],
                start=(ci == 0), stop=(ci == 7)),
                wait=([rw_t, const_t, ctx.tok("vector")] if ci == 0 else []),
                inc=([mm.inc()] if ci == 7 else []))
        l_mm = mm.tgt()
        lg_t = ctx.op("vector", lambda e: e.tensor_copy(lg[:], b[0][0:8, :]), wait=[l_mm])
        for i in range(4):
            ctx.op("tensor", lambda e, i=i: e.transpose(
                b[1][0:128, 0:8], lg[:, i * 128:(i + 1) * 128], idf[0:8, 0:8]),
                wait=[lg_t] if i == 0 else [ctx.tok("vector")], inc=[mm.inc()])
            t_mm = mm.tgt()
            jj = 4 * tk + i
            ctx.op("vector", lambda e, jj=jj: e.tensor_copy(
                ltok[:, jj * 8:(jj + 1) * 8], b[1][0:128, 0:8]), wait=[t_mm])
    lt_t = ctx.tok("vector")

    l3 = bass.AP(ltok, 0, [[256, 128], [8, 32], [1, 8]])
    ctx.op("vector", lambda e, l3=l3: e.tensor_reduce(v1[:, :], l3, axis=AX.X, op=ALU.max), wait=[lt_t])
    v1b = bass.AP(v1, 0, [[32, 128], [1, 32], [0, 8]])
    ctx.op("vector", lambda e, l3=l3, v1b=v1b: e.tensor_tensor(mask1[:], l3, v1b, op=ALU.is_equal))
    ctx.op("vector", lambda e: e.tensor_scalar(m9[:], mask1[:], -1e9, None, op0=ALU.mult))
    ctx.op("vector", lambda e: e.tensor_tensor(l2[:], ltok[:], m9[:], op=ALU.add))
    l23 = bass.AP(l2, 0, [[256, 128], [8, 32], [1, 8]])
    ctx.op("vector", lambda e, l23=l23: e.tensor_reduce(v2[:, :], l23, axis=AX.X, op=ALU.max))
    v2b = bass.AP(v2, 0, [[32, 128], [1, 32], [0, 8]])
    g_t = ctx.op("vector", lambda e, l23=l23, v2b=v2b: e.tensor_tensor(mask2[:], l23, v2b, op=ALU.is_equal))
    ctx.op("scalar", lambda e: e.activation(g1s[:], v1[:], AF.Sigmoid), wait=[g_t])
    sg_t = ctx.op("scalar", lambda e: e.activation(g2s[:], v2[:], AF.Sigmoid))
    ctx.op("vector", lambda e: e.tensor_tensor(den[:], g1s[:], g2s[:], op=ALU.add), wait=[sg_t])
    ctx.op("vector", lambda e: e.tensor_scalar(den[:], den[:], 1e-10, None, op0=ALU.add))
    ctx.op("vector", lambda e: e.reciprocal(rec[:], den[:]))
    ctx.op("vector", lambda e: e.tensor_tensor(gn1[:], g1s[:], rec[:], op=ALU.mult))
    ctx.op("vector", lambda e: e.tensor_tensor(gn2[:], g2s[:], rec[:], op=ALU.mult))
    gn1b = bass.AP(gn1, 0, [[32, 128], [1, 32], [0, 8]])
    gn2b = bass.AP(gn2, 0, [[32, 128], [1, 32], [0, 8]])
    ctx.op("vector", lambda e, gn1b=gn1b: e.tensor_tensor(mask1[:], mask1[:], gn1b, op=ALU.mult))
    ctx.op("vector", lambda e, gn2b=gn2b: e.tensor_tensor(mask2[:], mask2[:], gn2b, op=ALU.mult))
    ge_t = ctx.op("vector", lambda e: e.tensor_tensor(getok[:], mask1[:], mask2[:], op=ALU.add))
    for e_i in range(8):
        src = bass.AP(getok, e_i, [[256, 128], [8, 32]])
        ctx.op("tensor", lambda e, src=src: e.transpose(b[1][0:32, 0:128], src, idf[:, :]),
               wait=[ge_t] + ([stv.tgt()] if e_i > 0 else [ctx.tok("vector")]), inc=[mm.inc()])
        t_t = mm.tgt()
        c_t = ctx.op("vector", lambda e: e.tensor_copy(getr[:], b[1][0:32, 0:128]), wait=[t_t])
        ctx.op("sync", lambda e, e_i=e_i: e.dma_start(
            out=ge1_dram[e_i:e_i + 1, :], in_=getr[:]), wait=[c_t], inc=[stv.inc(16)])
    ge_st = stv.tgt()

    wg_r = io["l1_wg"].rearrange("g (ct p) f -> g p ct f", p=128)
    wu_r = io["l1_wu"].rearrange("g (ct p) f -> g p ct f", p=128)
    for e_i in range(8):
        ctx.op("sync", lambda e, e_i=e_i: e.dma_start(out=geR[:], in_=ge1_dram[e_i:e_i + 1, :]),
               wait=[ge_st, mm.at(mm.n)], inc=[ld.inc(16)])
        geR_t = ld.tgt()
        for tk in range(8):
            ctx.op("tensor", lambda e, tk=tk: e.matmul(
                b[6][:, :], o128[0:1, :], geR[0:1, tk * 512:(tk + 1) * 512],
                start=True, stop=True),
                wait=[geR_t, ctx.tok("vector")], inc=[mm.inc()])
            g_mm = mm.tgt()
            ctx.op("vector", lambda e, tk=tk: e.tensor_copy(
                gebc[:, tk * 512:(tk + 1) * 512], b[6][:, :]), wait=[g_mm])
        gebc_t = ctx.tok("vector")
        for ft in range(16):
            for wi, (w_r, bank0) in enumerate(((wg_r, 0), (wu_r, 2))):
                dbw = wi
                ctx.op("gpsimd", lambda e, e_i=e_i, ft=ft, dbw=dbw, w_r=w_r: e.dma_start(
                    out=bass.AP(wslab, dbw * 1024, [[2048, 128], [128, 8], [1, 128]]),
                    in_=w_r[e_i, :, :, 128 * ft:128 * (ft + 1)]),
                    wait=[mm.at(mm.n)], inc=[gld.inc(16)])
            w_t = gld.tgt()
            for tk in range(8):
                for wi, bank0 in ((0, 0), (1, 2)):
                    bk_i = bank0 + (tk % 2)
                    fw = [w_t, const_t, ctx.tok("scalar"), ctx.tok("vector")]
                    for ci in range(8):
                        ctx.op("tensor", lambda e, wi=wi, ci=ci, tk=tk, bk_i=bk_i: e.matmul(
                            b[bk_i][:, :], wslab[:, wi * 1024 + ci * 128:wi * 1024 + (ci + 1) * 128],
                            xn[:, ci * 4096 + tk * 512: ci * 4096 + (tk + 1) * 512],
                            start=(ci == 0), stop=(ci == 7)),
                            wait=(fw if ci == 0 else []),
                            inc=([mm.inc()] if ci == 7 else []))
                mm2_t = mm.tgt()
                bkg = 0 + (tk % 2)
                bku = 2 + (tk % 2)
                ctx.op("scalar", lambda e, tk=tk, bkg=bkg: e.activation(
                    sg[:, tk * 512:(tk + 1) * 512], b[bkg][:, :], AF.Silu),
                    wait=[mm2_t] + ([ctx.tok("vector")] if tk == 0 else []))
                su_t = ctx.tok("scalar")
                ctx.op("vector", lambda e, tk=tk, bku=bku: e.tensor_tensor(
                    ug[:, tk * 512:(tk + 1) * 512], b[bku][:, :], gebc[:, tk * 512:(tk + 1) * 512],
                    op=ALU.mult), wait=[mm2_t, gebc_t])
                h_t = ctx.op("vector", lambda e, tk=tk: e.tensor_tensor(
                    hsb[:, tk * 512:(tk + 1) * 512], sg[:, tk * 512:(tk + 1) * 512],
                    ug[:, tk * 512:(tk + 1) * 512], op=ALU.mult), wait=[su_t, stv.at(stv.n)])
            hh_t = ctx.tok("vector")
            ctx.op("sync", lambda e, e_i=e_i, ft=ft: e.dma_start(
                out=h_arr[e_i * H1 + 128 * ft: e_i * H1 + 128 * (ft + 1), :], in_=hsb[:, :]),
                wait=[hh_t], inc=[stv.inc(16)])

    wd_r = io["l1_wd"].rearrange("g (ct p) f -> g p ct f", p=128)
    h_r = h_arr.rearrange("(r p) t -> p r t", p=128)
    s1_r = s1_dram.rearrange("(dt p) t -> p dt t", p=128)
    x1_r = x1_dram.rearrange("(dt p) t -> p dt t", p=128)
    o_r = out_dram.rearrange("(dt p) t -> p dt t", p=128)
    for tk in range(8):
        first = True
        acc_done = {}
        for e_i in range(8):
            for fc in range(16):
                kk2 = e_i * 16 + fc
                dbw = kk2 % 2
                lw = [acc_done[kk2 - 2]] if kk2 >= 2 else [mm.at(mm.n)]
                ctx.op("gpsimd", lambda e, e_i=e_i, fc=fc, dbw=dbw: e.dma_start(
                    out=bass.AP(wslab, dbw * 1024, [[2048, 128], [128, 8], [1, 128]]),
                    in_=wd_r[e_i, :, fc, :]),
                    wait=lw, inc=[gld.inc(16)])
                ctx.op("sync", lambda e, e_i=e_i, fc=fc, tk=tk, dbw=dbw: e.dma_start(
                    out=hld[:, dbw * 512:(dbw + 1) * 512],
                    in_=h_r[:, e_i * 16 + fc, tk * 512:(tk + 1) * 512]),
                    wait=lw + [stv.at(stv.n)], inc=[ld.inc(16)])
                wm_t = gld.tgt()
                hh_t = ld.tgt()
                last = (e_i == 7 and fc == 15)
                for dt in range(8):
                    fw2 = [wm_t, hh_t]
                    if first and dt == 0:
                        fw2 += [ctx.tok("scalar"), ctx.tok("vector")]
                    ctx.op("tensor", lambda e, dbw=dbw, dt=dt, first=first, last=last: e.matmul(
                        b[dt][:, :], wslab[:, dbw * 1024 + dt * 128:dbw * 1024 + (dt + 1) * 128],
                        hld[:, dbw * 512:(dbw + 1) * 512],
                        start=first, stop=last),
                        wait=(fw2 if dt == 0 else []),
                        inc=([mm.inc()] if dt == 7 else []))
                acc_done[kk2] = mm.tgt()
                first = False
        y_mm = mm.tgt()
        for dt in range(8):
            dbs = dt % 2
            ctx.op("sync", lambda e, dt=dt, tk=tk, dbs=dbs: e.dma_start(
                out=s1sb[:, dbs * 512:(dbs + 1) * 512], in_=s1_r[:, dt, tk * 512:(tk + 1) * 512]),
                wait=[ctx.tok("vector")], inc=[ld.inc(16)])
            ctx.op("sync", lambda e, dt=dt, tk=tk, dbs=dbs: e.dma_start(
                out=x1sb[:, dbs * 512:(dbs + 1) * 512], in_=x1_r[:, dt, tk * 512:(tk + 1) * 512]),
                inc=[ld.inc(16)])
            sx_t = ld.tgt()
            ctx.op("scalar", lambda e, dt=dt: e.activation(
                g1v[:, :], b[dt][:, :], AF.Silu), wait=[y_mm])
            gs_t = ctx.tok("scalar")
            ctx.op("vector", lambda e, dbs=dbs: e.tensor_tensor(
                t1sb[:, :], s1sb[:, dbs * 512:(dbs + 1) * 512], g1v[:, :], op=ALU.mult),
                wait=[gs_t, sx_t])
            o_t = ctx.op("vector", lambda e, dbs=dbs: e.tensor_tensor(
                osb[:, :], t1sb[:, :], x1sb[:, dbs * 512:(dbs + 1) * 512], op=ALU.add))
            ctx.op("sync", lambda e, dt=dt, tk=tk: e.dma_start(
                out=o_r[:, dt, tk * 512:(tk + 1) * 512], in_=osb[:, :]),
                wait=[o_t], inc=[stv.inc(16)])
            ctx.wait("scalar", [stv.tgt()])
            ctx.wait("vector", [stv.tgt()])
    ctx.wait("sync", [stv.tgt()])
    return stv


def emit_launch1(ctx, debug=False):
    io = declare_common(ctx)
    xn0 = ctx.dram("xn0T", (D, T), BF16, kind="ExternalOutput").ap()
    ge0 = ctx.dram("ge0", (8, T), F32, kind="ExternalOutput").ap()
    thr_dbg = ctx.dram("thr0", (1, 8), F32, kind="ExternalOutput").ap()
    dbg = None
    if debug:
        dbg = {
            "mids": ctx.dram("dbg_mids", (40, 8), F32, kind="ExternalOutput").ap(),
            "cnts": ctx.dram("dbg_cnts", (40, 8), F32, kind="ExternalOutput").ap(),
            "rawcnts": ctx.dram("dbg_rawcnts", (40, 8), F32, kind="ExternalOutput").ap(),
            "ltok": ctx.dram("dbg_ltok", (128, 2048), F32, kind="ExternalOutput").ap(),
            "smd": ctx.dram("dbg_sm", (128, 2048), F32, kind="ExternalOutput").ap(),
            "sm2d": ctx.dram("dbg_sm2", (128, 2048), F32, kind="ExternalOutput").ap(),
        }
    src = io["xT_all"][:, 0:T]
    emit_ln(ctx, io, src, xn0, tag="ln0")
    emit_router0(ctx, io, ge0, thr_dbg=thr_dbg, dbg=dbg, tag="r0")


def emit_launch2(ctx, max_pairs=8, phase=5, subp=3):
    names = ["l0_wqT", "l0_wkT", "l0_wvT", "l0_woT", "l0_wpkT", "l0_wpqT",
             "l0_bq", "l0_bk", "l0_bv", "l0_bo", "l0_bpk", "l0_bpq",
             "l0_relT", "l0_rel_rT", "c_id_bf", "c_ones1x128_f32",
             "c_ones_row_bf", "c_zeros_row_bf"]
    io = declare_common(ctx, names)
    xn0 = ctx.dram("xn0T_in", (D, T), BF16, kind="ExternalInput").ap()
    s0 = ctx.nc.dram_tensor("s0T", [D, T], F32, kind="ExternalOutput").ap()
    emit_l0attn(ctx, io, xn0, s0, max_pairs=max_pairs, phase=phase, subp=subp)


def emit_wtil(ctx, io, wtil_dram, tag="wt"):
    gld = ctx.sem(tag + "_gld")
    stv = ctx.sem(tag + "_st")
    sa = ctx.sb(tag + "_sa", (128, 1024), BF16)
    mb = ctx.sb(tag + "_mb", (128, 1024), BF16)
    wt = ctx.sb(tag + "_wt", (128, 1024), BF16)
    for ct in range(8):
        ctx.op("gpsimd", lambda e, ct=ct: e.dma_start(
            out=sa[:, :], in_=io["l1_sfwT"][128 * ct:128 * (ct + 1), :]),
            wait=[stv.at(stv.n)], inc=[gld.inc(16)])
        ctx.op("gpsimd", lambda e, ct=ct: e.dma_start(
            out=mb[:, :], in_=io["l1_maskT"][128 * ct:128 * (ct + 1), :]), inc=[gld.inc(16)])
        g_t = gld.tgt()
        w_t = ctx.op("vector", lambda e: e.tensor_tensor(wt[:], sa[:], mb[:], op=ALU.mult),
                     wait=[g_t, stv.at(stv.n)])
        ctx.op("sync", lambda e, ct=ct: e.dma_start(
            out=wtil_dram[128 * ct:128 * (ct + 1), :], in_=wt[:, :]), wait=[w_t], inc=[stv.inc(16)])
    ctx.wait("sync", [stv.tgt()])
    return stv


def emit_launch3(ctx):
    names = ["xT_all", "l0_wup", "l0_bup", "l0_wdn", "l0_bdn", "c_ones1x128_f32"]
    io = declare_common(ctx, names)
    xn0 = ctx.dram("xn0T_in", (D, T), BF16, kind="ExternalInput").ap()
    ge0 = ctx.dram("ge0_in", (8, T), F32, kind="ExternalInput").ap()
    s0 = ctx.dram("s0T_in", (D, T), F32, kind="ExternalInput").ap()
    x1 = ctx.nc.dram_tensor("x1T", [D, T], F32, kind="ExternalOutput").ap()
    emit_l0moe(ctx, io, xn0, ge0, s0, io["xT_all"][:, 0:T], x1)


def emit_launch4a(ctx):
    names = ["c_invD_bf", "c_ones1x128_f32"]
    io = declare_common(ctx, names)
    x1 = ctx.dram("x1T_in", (D, T), F32, kind="ExternalInput").ap()
    xn1b = ctx.nc.dram_tensor("xn1T", [D, T], BF16, kind="ExternalOutput").ap()
    xn132 = ctx.nc.dram_tensor("xn1T32", [D, T], F32, kind="ExternalOutput").ap()
    emit_ln(ctx, io, x1, xn1b, dst_f32=xn132, tag="ln1")


def emit_launch4b(ctx):
    names = ["l1_g1", "l1_b1", "c_invD_bf", "c_ones1x128_f32"]
    io = declare_common(ctx, names)
    xn132 = ctx.dram("xn1T32_in", (D, T), F32, kind="ExternalInput").ap()
    t1 = ctx.nc.dram_tensor("t1T", [D, T], BF16, kind="ExternalOutput").ap()
    emit_ln(ctx, io, xn132, t1, gain=io["l1_g1"], bias=io["l1_b1"], tag="lna1")


def emit_launch4c(ctx):
    names = ["l1_sfwT", "l1_maskT", "l1_sfb"]
    io = declare_common(ctx, names)
    t1 = ctx.dram("t1T_in", (D, T), BF16, kind="ExternalInput").ap()
    xn132 = ctx.dram("xn1T32_in", (D, T), F32, kind="ExternalInput").ap()
    a1 = ctx.nc.dram_tensor("a1T", [D, T], F32, kind="ExternalOutput").ap()
    wtil = ctx.dram("wtilT", (D, D), BF16).ap()
    wt_st = emit_wtil(ctx, io, wtil)
    emit_linear(ctx, io, wtil, t1, D, D, a1, F32, bias_cols=io["l1_sfb"],
                act=None, resid32=xn132, extra_wait=[wt_st.tgt()], tag="sf")


def emit_launch5a(ctx):
    names = ["l1_g2", "l1_b2", "c_invD_bf", "c_ones1x128_f32"]
    io = declare_common(ctx, names)
    a1 = ctx.dram("a1T_in", (D, T), F32, kind="ExternalInput").ap()
    t2 = ctx.nc.dram_tensor("t2T", [D, T], BF16, kind="ExternalOutput").ap()
    emit_ln(ctx, io, a1, t2, gain=io["l1_g2"], bias=io["l1_b2"], tag="lna2")


def emit_launch5b(ctx):
    names = ["l1_wqT", "l1_wkT", "l1_wvT", "l1_woT", "l1_bq", "l1_bk", "l1_bv", "l1_bo",
             "c_id_bf", "c_ones1x128_f32", "c_causal"]
    io = declare_common(ctx, names)
    t2 = ctx.dram("t2T_in", (D, T), BF16, kind="ExternalInput").ap()
    a1 = ctx.dram("a1T_in", (D, T), F32, kind="ExternalInput").ap()
    a2 = ctx.nc.dram_tensor("a2T", [D, T], F32, kind="ExternalOutput").ap()
    emit_l1attn_core(ctx, io, t2, a1, a2)


def emit_launch6a(ctx):
    names = ["l1_g3", "l1_b3", "c_invD_bf", "c_ones1x128_f32"]
    io = declare_common(ctx, names)
    a2 = ctx.dram("a2T_in", (D, T), F32, kind="ExternalInput").ap()
    t3 = ctx.nc.dram_tensor("t3T", [D, T], BF16, kind="ExternalOutput").ap()
    emit_ln(ctx, io, a2, t3, gain=io["l1_g3"], bias=io["l1_b3"], tag="lna3")


def emit_launch6b(ctx):
    names = ["l1_ff1T", "l1_ff1b", "l1_ff2T", "l1_ff2b"]
    io = declare_common(ctx, names)
    t3 = ctx.dram("t3T_in", (D, T), BF16, kind="ExternalInput").ap()
    a2 = ctx.dram("a2T_in", (D, T), F32, kind="ExternalInput").ap()
    s1 = ctx.nc.dram_tensor("s1T", [D, T], F32, kind="ExternalOutput").ap()
    emit_l1ff(ctx, io, t3, a2, s1)


def emit_launch7(ctx):
    names = ["l1_rwT", "l1_wg", "l1_wu", "l1_wd", "c_id_f32", "c_ones1x128_f32"]
    io = declare_common(ctx, names)
    xn1 = ctx.dram("xn1T_in", (D, T), BF16, kind="ExternalInput").ap()
    s1 = ctx.dram("s1T_in", (D, T), F32, kind="ExternalInput").ap()
    x1 = ctx.dram("x1T_in", (D, T), F32, kind="ExternalInput").ap()
    out = ctx.nc.dram_tensor("outT", [D, T], F32, kind="ExternalOutput").ap()
    emit_l1moe(ctx, io, xn1, s1, x1, out)



def _run(nc, in_maps):
    return run_bass_kernel_spmd(nc, in_maps, core_ids=list(range(NCORES))).results


def kernel(**inputs):
    S, per_core = prep_inputs(inputs)
    cores = range(NCORES)

    def maps(extra_per_core=None, names=None):
        ms = []
        for c in cores:
            m = {}
            src = dict(S)
            src.update(per_core[c])
            if extra_per_core is not None:
                src.update(extra_per_core[c])
            if names is None:
                m = src
            else:
                m = {k: src[k] for k in names}
            ms.append(m)
        return ms

    def runl(emit, extra=None):
        nc, names = build_launch(emit)
        return _run(nc, maps(extra, names))

    _DBG.clear()
    r1 = runl(emit_launch1)
    r2 = runl(emit_launch2, [{"xn0T_in": r1[c]["xn0T"]} for c in cores])
    r3 = runl(emit_launch3, [{"xn0T_in": r1[c]["xn0T"], "ge0_in": r1[c]["ge0"],
                              "s0T_in": r2[c]["s0T"]} for c in cores])
    r4a = runl(emit_launch4a, [{"x1T_in": r3[c]["x1T"]} for c in cores])
    r4b = runl(emit_launch4b, [{"xn1T32_in": r4a[c]["xn1T32"]} for c in cores])
    r4 = runl(emit_launch4c, [{"t1T_in": r4b[c]["t1T"],
                               "xn1T32_in": r4a[c]["xn1T32"]} for c in cores])
    r5a = runl(emit_launch5a, [{"a1T_in": r4[c]["a1T"]} for c in cores])
    r5 = runl(emit_launch5b, [{"t2T_in": r5a[c]["t2T"], "a1T_in": r4[c]["a1T"]} for c in cores])
    r6a = runl(emit_launch6a, [{"a2T_in": r5[c]["a2T"]} for c in cores])
    r6 = runl(emit_launch6b, [{"t3T_in": r6a[c]["t3T"], "a2T_in": r5[c]["a2T"]} for c in cores])
    r7 = runl(emit_launch7, [{"xn1T_in": r4a[c]["xn1T"], "s1T_in": r6[c]["s1T"],
                              "x1T_in": r3[c]["x1T"]} for c in cores])
    _DBG.update(dict(r1=r1, r2=r2, r3=r3, r4=r4, r4a=r4a, r5=r5, r6=r6, r7=r7))
    out = np.stack([np.asarray(r7[c]["outT"]).T for c in cores], axis=0)
    return out.astype(np.float32)


_DBG = {}
